# version 03
# baseline (speedup 1.0000x reference)
import numpy as np
import concourse.bass as bass
import concourse.mybir as mybir
from concourse.bass_utils import run_bass_kernel_spmd

F32 = mybir.dt.float32
BF16 = mybir.dt.bfloat16
AF = mybir.ActivationFunctionType
ALU = mybir.AluOpType
AX = mybir.AxisListType

NCORES = 8
TOK = 2048
NFC = 8
EPS = 1e-5
DEX = 4096
NE = 8
SCL_F = 1.0 / 32.0
SCL_A = 0.125
KV_ROWS_F = 2048
KV_ROWS_A = 128
SB_BASE = 20480


class Sem:
    __slots__ = ("h", "base", "n")

    def __init__(self, h, base):
        self.h = h
        self.base = base
        self.n = base

    def inc(self, instr, k=1):
        instr.then_inc(self.h, k)
        self.n += k
        return instr

    def wait(self, eng, val=None):
        eng.wait_ge(self.h, self.n if val is None else self.base + val)

    def wait_abs(self, eng, val):
        eng.wait_ge(self.h, val)


ENGS = ("tensor", "vector", "scalar", "sync", "gpsimd")

_SKIP_LAST = {"nop", "engine_nop", "wait_ge", "wait_op", "sem_inc", "drain"}


class EngWrap:

    def __init__(self, eng):
        self._e = eng
        self.last = None

    def __getattr__(self, k):
        a = getattr(self._e, k)
        if callable(a) and not k.startswith("_"):
            def w(*args, __a=a, __k=k, **kw):
                r = __a(*args, **kw)
                if __k not in _SKIP_LAST and hasattr(r, "then_inc"):
                    self.last = r
                return r
            return w
        return a


class MK:
    def __init__(self, depth_stop=3, dbg=False, attn_stop=99):
        self.dbg = dbg
        self.attn_stop = attn_stop
        self.dbg_outs = {}
        nc = bass.Bass("TRN2", target_bir_lowering=False,
                       detect_race_conditions=False)
        self.nc = nc
        self.depth_stop = depth_stop
        self.pending = []

        di = lambda n, s: nc.dram_tensor(n, s, F32, kind="ExternalInput")
        self.xT = di("xT", [1024, TOK])
        self.fWq = di("fWq", [1024, 1024]); self.fWk = di("fWk", [1024, 1024])
        self.fWv = di("fWv", [1024, 1024]); self.fWo = di("fWo", [1024, 1024])
        self.fbq = di("fbq", [1024]); self.fbk = di("fbk", [1024])
        self.fbv = di("fbv", [1024]); self.fbo = di("fbo", [1024])
        self.fWg = di("fWg", [1024, 8]); self.fbg = di("fbg", [8])
        self.fW1 = di("fW1", [NE, 1024, DEX]); self.fb1 = di("fb1", [NE, DEX])
        self.fW2 = di("fW2", [NE, DEX, 1024]); self.fb2 = di("fb2", [NE, 1024])
        self.aWq = di("aWq", [1024, 512]); self.aWkv = di("aWkv", [1024, 128])
        self.aWo = di("aWo", [512, 1024])
        self.anw = di("anw", [1024]); self.anb = di("anb", [1024])
        self.aonw = di("aonw", [1024]); self.aonb = di("aonb", [1024])
        self.eye = di("eye", [128, 128])
        self.yT = nc.dram_tensor("yT", [1024, TOK], F32, kind="ExternalOutput")

        self.spx = nc.dram_tensor("spx", [128, 8 * TOK], F32)
        self.spu = nc.dram_tensor("spu", [128, 8 * TOK], F32)
        self.kvsl_f = nc.dram_tensor("kvsl_f", [KV_ROWS_F, TOK], BF16)
        self.kvslK = nc.dram_tensor("kvslK", [4096, 512], BF16)
        self.kvgK = [nc.dram_tensor(f"kvgK{q}", [NCORES * 1024, 512], BF16,
                                    addr_space="Shared") for q in range(4)]
        self.kvgV = [nc.dram_tensor(f"kvgV{q}", [NCORES * 256, TOK], BF16,
                                    addr_space="Shared") for q in range(4)]
        self.kvsl_a = nc.dram_tensor("kvsl_a", [KV_ROWS_A, TOK], BF16)
        self.kvg_a = nc.dram_tensor("kvg_a", [NCORES * KV_ROWS_A, TOK], BF16,
                                    addr_space="Shared")
        self.rowb = nc.dram_tensor("rowb", [16, 128], F32)

        def at(name, shape, dt, off):
            return nc.alloc_sbuf_tensor_at(name, shape, dt, offset=SB_BASE + off)
        self.A = at("A", [128, 8 * TOK], F32, 0)
        self.Ab = at("Ab", [128, 16 * TOK], BF16, 0)
        self.Bb = at("Bb", [128, 8 * TOK], BF16, 65536)
        self.B65 = at("B65", [65, 8 * TOK], BF16, 65536)
        self.Bf = at("Bf", [128, 4 * TOK], F32, 65536)
        self.Cb = at("Cb", [128, 8 * TOK], BF16, 98304)
        self.C65 = at("C65", [65, 8 * TOK], BF16, 98304)
        self.Db = at("Db", [128, 16896], BF16, 131072)
        self.Df = at("Df", [128, 8448], F32, 131072)
        self.Drows = at("Drows", [1, 8192], F32, 131072)
        self.Eb = at("Eb", [128, 8192], BF16, 164864)
        self.bvrow = at("bvrow", [1, 1024], BF16, 164864)
        self.Ff = at("Ff", [128, 4096], F32, 181248)
        self.Fb = at("Fb", [128, 8192], BF16, 181248)
        o = 197632
        def park(name, shape, dt):
            nonlocal o
            t = at(name, shape, dt, o)
            esz = 2 if dt == BF16 else 4
            o += ((shape[-1] * esz + 31) // 32) * 32
            return t
        self.eye_f = park("eye_f", [128, 128], F32)
        self.bq_c = park("bq_c", [128, 8], F32)
        self.bk_c = park("bk_c", [128, 8], F32)
        self.bo_c = park("bo_c", [128, 8], F32)
        self.anw_c = park("anw_c", [128, 8], F32)
        self.anb_c = park("anb_c", [128, 8], F32)
        self.aonw_c = park("aonw_c", [128, 8], F32)
        self.aonb_c = park("aonb_c", [128, 8], F32)
        self.b1T = park("b1T", [128, 8 * 32], F32)
        self.b2sb = park("b2sb", [8, 1024], BF16)
        self.wg_sb = park("wg_sb", [128, 64], BF16)
        self.bgrow = park("bgrow", [1, 8], BF16)
        self.gTM = park("gTM", [128, 16 * 8], F32)
        self.ones_b = park("ones_b", [128, 1], BF16)
        self.ones_f = park("ones_f", [128, 1], F32)
        self.onesr_f = park("onesr_f", [1, 128], F32)
        self.onesr_b = park("onesr_b", [1, 512], BF16)
        self.Mrow = park("Mrow", [1, 16], F32)
        self.tr16 = park("tr16", [16, 128], F32)
        self.negC = park("negC", [1, TOK], BF16)
        self.ones2d = park("ones2d", [128, 64], F32)
        self.eps_c = park("eps_c", [128, 1], F32)
        assert SB_BASE + o <= 229344, f"sbuf overflow: {SB_BASE + o}"

        self.P0 = nc.alloc_psum_tensor("P0", [128, 2048], F32)
        self.P1 = nc.alloc_psum_tensor("P1", [128, 1024], F32)
        self.P2 = nc.alloc_psum_tensor("P2", [128, 1024], F32)

        self._semct = 0
        self._free = {"eng": [], "dmah": [], "dmas": [], "cc": []}
        self._hw_n = {}
        self._scope = []
        self.s_st = self.sem("st", "dmah")
        self._scope = []
        self.block = None

    def attn_trunc(self, stage):
        if self.attn_stop > stage:
            return False

        def f(sync):
            i = sync.dma_start(out=self.yT[:],
                               in_=self.spx[:, 0:TOK].rearrange("p t -> p t"))
            self.s_st.inc(i, 16)
        def f2(sync):
            i = sync.dma_start(out=self.yT.rearrange("(c p) t -> p c t", p=128),
                               in_=self.spx.rearrange("p (c t) -> p c t", t=TOK))
            self.s_st.inc(i, 16)
        self.on("sync", f2)
        self.bar()
        return True

    def dump_dram(self, name, src_ap, shape, dt):
        if not self.dbg:
            return
        t = self.nc.dram_tensor(name, shape, dt, kind="ExternalOutput")
        self.dbg_outs[name] = t

        def f(sync):
            i = sync.dma_start(out=t[:], in_=src_ap)
            self.s_st.inc(i, 16)
        self.on("sync", f)

    def dump_sbuf(self, name, src_ap, shape, dt):
        if not self.dbg:
            return
        t = self.nc.dram_tensor(name, shape, dt, kind="ExternalOutput")
        self.dbg_outs[name] = t

        def f(sync):
            i = sync.dma_start(out=t[:], in_=src_ap)
            self.s_st.inc(i, 16)
        self.on("sync", f)
        self.bar()

    def sem(self, tag, kind="eng"):
        self._semct += 1
        free = self._free[kind]
        if free:
            h = free.pop()
        else:
            h = self.nc.alloc_semaphore(f"s{self._semct}_{tag}")
            self._hw_n[h] = 0
        s = Sem(h, self._hw_n[h])
        self._scope.append((kind, s))
        return s

    def release_scope(self):
        for kind, s in self._scope:
            self._hw_n[s.h] = s.n
            self._free[kind].append(s.h)
        self._scope = []

    def on(self, engname, f):
        if getattr(self, "_scope_name", None):
            with self.nc.named_scope(self._scope_name):
                getattr(self.block, engname)(lambda eng: f(EngWrap(eng)))
        else:
            getattr(self.block, engname)(lambda eng: f(EngWrap(eng)))

    def scope(self, name):
        self._scope_name = name

    def done(self, sem):
        self.pending.append((sem, sem.n))

    def bar(self):
        pend = [(s, v) for (s, v) in self.pending if v > 0]
        self.pending = []
        for c in ENGS:
            def f(eng, pend=pend):
                for s, v in pend:
                    s.wait_abs(eng, v)
            self.on(c, f)

    def emit_setup(self):
        s_ld = self.sem("ld", "dmah")
        s_tr = self.sem("tr", "dmah")
        s_g = self.sem("g", "dmas")
        s_gm = self.sem("gm")
        s_pe = self.sem("pe")
        s_dve = self.sem("dve")

        def f_sync(sync):
            s_ld.inc(sync.dma_start(out=self.A.rearrange("p (c t) -> p c t", t=TOK),
                                    in_=self.xT.rearrange("(c p) t -> p c t", p=128)), 16)
            s_ld.inc(sync.dma_start(out=self.eye_f[:], in_=self.eye[:]), 16)
        self.on("sync", f_sync)

        def f_g(g):
            s_g.inc(g.dma_start(out=self.b2sb[:], in_=self.fb2[:]), 16)
            s_g.inc(g.dma_start(out=self.wg_sb.rearrange("p (c e) -> p c e", e=8),
                                in_=self.fWg.rearrange("(c p) e -> p c e", p=128)), 16)
            s_g.inc(g.dma_start(out=self.bgrow[:], in_=self.fbg[None, :]), 16)
            s_g.inc(g.dma_start(out=self.bvrow[:], in_=self.fbv[None, :]), 16)
            g.memset(self.ones_b[:], 1.0)
            g.memset(self.ones_f[:], 1.0)
            g.memset(self.onesr_f[:], 1.0)
            g.memset(self.ones2d[:], 1.0)
            g.memset(self.eps_c[:], EPS)
            s_gm.inc(g.memset(self.onesr_b[:], 1.0), 1)
        self.on("gpsimd", f_g)

        vecs = [(self.fbq, self.bq_c), (self.fbk, self.bk_c), (self.fbo, self.bo_c),
                (self.anw, self.anw_c), (self.anb, self.anb_c),
                (self.aonw, self.aonw_c), (self.aonb, self.aonb_c)]
        for src, dst in vecs:
            def f_s(sync, src=src):
                s_pe.wait(sync)
                s_tr.inc(sync.dma_start(out=self.tr16[0:8, :],
                                        in_=src.rearrange("(c p) -> c p", p=128)), 16)
            self.on("sync", f_s)

            def f_t(t):
                s_ld.wait(t)
                s_tr.wait(t)
                s_dve.wait(t)
                s_pe.inc(t.transpose(self.P1[:, 0:8], self.tr16[0:8, :],
                                     self.eye_f[0:8, 0:8]))
            self.on("tensor", f_t)

            def f_v(v, dst=dst):
                s_pe.wait(v)
                s_dve.inc(v.tensor_copy(dst[:], self.P1[:, 0:8]))
            self.on("vector", f_v)

        for e in range(NE):
            for half in range(2):
                def f_s(sync, e=e, half=half):
                    s_pe.wait(sync)
                    i = sync.dma_start(
                        out=self.tr16[:],
                        in_=self.fb1[e].rearrange("(a b) -> a b", b=128)[16 * half:16 * (half + 1), :])
                    s_tr.inc(i, 16)
                self.on("sync", f_s)

                def f_t(t):
                    s_tr.wait(t)
                    s_dve.wait(t)
                    s_pe.inc(t.transpose(self.P1[:, 0:16], self.tr16[:],
                                         self.eye_f[0:16, 0:16]))
                self.on("tensor", f_t)

                def f_v(v, e=e, half=half):
                    s_pe.wait(v)
                    c0 = 32 * e + 16 * half
                    s_dve.inc(v.tensor_copy(self.b1T[:, c0:c0 + 16], self.P1[:, 0:16]))
                self.on("vector", f_v)

        for s in (s_ld, s_tr, s_g, s_gm, s_pe, s_dve):
            self.done(s)
        self.bar()
        self.release_scope()

    def emit_ln(self, mode):
        P0, P1, P2 = self.P0, self.P1, self.P2
        mu = self.Drows[:, 0:2048]
        sr = self.Drows[:, 2048:4096]
        tp = self.Drows[:, 4096:6144]
        s_c = self.sem("lncast")
        s_p = self.sem("lnpe")
        s_q = self.sem("lnsq")
        s_r = self.sem("lnrow")

        def f_cast(v):
            for fc in range(NFC):
                i = v.tensor_copy(self.Bb[:, 2048 * fc:2048 * (fc + 1)],
                                  self.A[:, 2048 * fc:2048 * (fc + 1)])
            s_c.inc(i)
        self.on("vector", f_cast)

        def f_sumx(t):
            s_c.wait(t)
            for fc in range(NFC):
                for qh in range(2):
                    ps = P1 if qh == 0 else P2
                    for sl in range(2):
                        c0 = 2048 * fc + 1024 * qh + 512 * sl
                        t.matmul(ps[0:1, 512 * sl:512 * (sl + 1)], self.ones_b[:],
                                 self.Bb[:, c0:c0 + 512],
                                 start=(fc == 0), stop=(fc == NFC - 1),
                                 skip_group_check=True)
            s_p.inc(t.last)
        self.on("tensor", f_sumx)

        def f_muev(v):
            s_p.wait(v, 1)
            v.tensor_scalar(mu[:, 0:1024], P1[0:1, :], 1.0 / 1024.0, None, ALU.mult)
            i = v.tensor_scalar(mu[:, 1024:2048], P2[0:1, :], 1.0 / 1024.0, None, ALU.mult)
            s_r.inc(i)
        self.on("vector", f_muev)

        def f_sq(s):
            s_c.wait(s)
            for fc in range(NFC):
                if fc >= 2:
                    s_p.wait(s, fc)
                i = s.activation(self.Fb[:, 2048 * (fc % 2):2048 * (fc % 2 + 1)],
                                 self.Bb[:, 2048 * fc:2048 * (fc + 1)], AF.Square)
                s_q.inc(i)
        self.on("scalar", f_sq)

        def f_sumsq(t):
            s_r.wait(t, 1)
            for fc in range(NFC):
                s_q.wait(t, fc + 1)
                for qh in range(2):
                    ps = P1 if qh == 0 else P2
                    for sl in range(2):
                        c0 = 2048 * (fc % 2) + 1024 * qh + 512 * sl
                        t.matmul(ps[0:1, 512 * sl:512 * (sl + 1)], self.ones_b[:],
                                 self.Fb[:, c0:c0 + 512],
                                 start=(fc == 0), stop=(fc == NFC - 1),
                                 skip_group_check=True)
                s_p.inc(t.last)
        self.on("tensor", f_sumsq)

        def f_var(v):
            s_p.wait(v, 1 + NFC)
            v.tensor_scalar(sr[:, 0:1024], P1[0:1, :], 1.0 / 1024.0, None, ALU.mult)
            v.tensor_scalar(sr[:, 1024:2048], P2[0:1, :], 1.0 / 1024.0, None, ALU.mult)
            v.tensor_mul(tp, mu, mu)
            i = v.tensor_sub(sr, sr, tp)
            s_r.inc(i)
        self.on("vector", f_var)

        def f_rs(s):
            s_r.wait(s, 2)
            i = s.activation(tp, sr, AF.Sqrt, bias=self.eps_c[0:1, :])
            s_q.inc(i)
        self.on("scalar", f_rs)

        def f_r(v):
            s_q.wait(v, NFC + 1)
            i = v.reciprocal(tp, tp)
            s_r.inc(i)
        self.on("vector", f_r)

        if mode == "double":
            def f_v2(v):
                s_r.wait(v, 3)
                v.tensor_mul(sr, sr, tp)
                i = v.tensor_mul(sr, sr, tp)
                s_r.inc(i)
            self.on("vector", f_v2)

            def f_rs2(s):
                s_r.wait(s, 4)
                i = s.activation(sr, sr, AF.Sqrt, bias=self.eps_c[0:1, :])
                s_q.inc(i)
            self.on("scalar", f_rs2)

            def f_r2(v):
                s_q.wait(v, NFC + 2)
                i = v.reciprocal(sr, sr)
                s_r.inc(i)
                s_r.wait(v, 5)
                i = v.tensor_mul(sr, sr, tp)
                s_r.inc(i)
            self.on("vector", f_r2)
        else:
            def f_cp(v):
                s_r.wait(v, 3)
                i = v.tensor_copy(sr, tp)
                s_r.inc(i)
            self.on("vector", f_cp)
        rows_n = s_r.n - s_r.base

        def f_bc(t):
            s_r.wait(t, rows_n)
            for sl in range(4):
                t.matmul(P0[:, 512 * sl:512 * (sl + 1)], self.onesr_f[:],
                         mu[:, 512 * sl:512 * (sl + 1)], start=True, stop=True)
            for sl in range(2):
                t.matmul(P1[:, 512 * sl:512 * (sl + 1)], self.onesr_f[:],
                         sr[:, 512 * sl:512 * (sl + 1)], start=True, stop=True)
                t.matmul(P2[:, 512 * sl:512 * (sl + 1)], self.onesr_f[:],
                         sr[:, 1024 + 512 * sl:1024 + 512 * (sl + 1)], start=True, stop=True)
            s_p.inc(t.last)
        self.on("tensor", f_bc)
        self.done(s_p)
        return s_p

    def emit_ffn(self, li, last):
        P0, P1, P2 = self.P0, self.P1, self.P2
        A, Ab, Bb, Bf, Cb = self.A, self.Ab, self.Bb, self.Bf, self.Cb
        Db, Df, Eb, Ff, Fb = self.Db, self.Df, self.Eb, self.Ff, self.Fb

        self.scope("ln")
        s_lnp = self.emit_ln("plain")
        lnp_n = s_lnp.n - s_lnp.base
        stage = Ff[:, 2048:4096]
        s_app = self.sem("app")
        s_spl = self.sem("spl", "dmah")

        def f_app(v):
            s_lnp.wait(v, lnp_n)
            for fc in range(NFC):
                xc = A[:, 2048 * fc:2048 * (fc + 1)]
                if fc > 0:
                    s_spl.wait(v, 32 * fc)
                i = v.tensor_sub(stage, xc, P0[:])
                s_app.inc(i)
                for qh in range(2):
                    ps = P1 if qh == 0 else P2
                    v.tensor_mul(stage[:, 1024 * qh:1024 * (qh + 1)],
                                 stage[:, 1024 * qh:1024 * (qh + 1)], ps[:])
                i = v.tensor_copy(Bb[:, 2048 * fc:2048 * (fc + 1)], stage)
                s_app.inc(i)
        self.on("vector", f_app)

        def f_spl(sync):
            for fc in range(NFC):
                s_app.wait(sync, 2 * fc + 1)
                s_spl.inc(sync.dma_start(out=self.spx[:, 2048 * fc:2048 * (fc + 1)],
                                         in_=A[:, 2048 * fc:2048 * (fc + 1)]), 16)
                s_app.wait(sync, 2 * fc + 2)
                s_spl.inc(sync.dma_start(out=self.spu[:, 2048 * fc:2048 * (fc + 1)],
                                         in_=stage), 16)
        self.on("sync", f_spl)
        self.done(s_app); self.done(s_spl)
        self.bar()
        self.dump_dram(f"dbg_u{li}", self.spu[:], [128, 8 * TOK], F32)
        self.dump_dram(f"dbg_x{li}", self.spx[:], [128, 8 * TOK], F32)

        self.scope("proj")
        WA = Db[:, 0:8192]
        WB = Db[:, 8192:16384]
        s_wl = self.sem("wl", "dmas")

        def f_wdma(g):
            i = g.dma_start(out=WA.rearrange("p (c n) -> p c n", n=1024),
                            in_=self.fWq.rearrange("(c p) n -> p c n", p=128))
            s_wl.inc(i, 16)
            i = g.dma_start(out=WB.rearrange("p (c n) -> p c n", n=1024),
                            in_=self.fWv.rearrange("(c p) n -> p c n", p=128))
            s_wl.inc(i, 16)
            i = g.dma_start(out=self.bvrow[:], in_=self.fbv[None, :])
            s_wl.inc(i, 16)
        self.on("gpsimd", f_wdma)

        s_qp = self.sem("qp")
        s_qe = self.sem("qe")

        def f_q(t):
            s_wl.wait(t, 16)
            for oc in range(NFC):
                if oc > 0:
                    s_qe.wait(t, oc)
                for sl in range(4):
                    for ic in range(NFC):
                        t.matmul(P0[:, 512 * sl:512 * (sl + 1)],
                                 WA[:, 1024 * ic + 128 * oc:1024 * ic + 128 * oc + 128],
                                 Bb[:, 2048 * ic + 512 * sl:2048 * ic + 512 * (sl + 1)],
                                 start=(ic == 0), stop=(ic == NFC - 1))
                s_qp.inc(t.last)
        self.on("tensor", f_q)

        def f_qe(s):
            for oc in range(NFC):
                s_qp.wait(s, oc + 1)
                i = s.activation(Cb[:, 2048 * oc:2048 * (oc + 1)], P0[:], AF.Identity,
                                 bias=self.bq_c[:, oc:oc + 1])
                s_qe.inc(i)
        self.on("scalar", f_qe)

        s_vp = self.sem("vp")
        s_ve = self.sem("ve")
        s_vs = self.sem("vs", "dmah")
        vstage = Fb[:, 0:1024]

        def f_v(t):
            s_wl.wait(t, 48)
            s_qe.wait(t, NFC)
            for tt in range(16):
                ps = P1 if tt % 2 == 0 else P2
                if tt >= 2:
                    s_ve.wait(t, tt - 1)
                for sl in range(2):
                    for ic in range(NFC):
                        t.matmul(ps[:, 512 * sl:512 * (sl + 1)],
                                 Bb[:, 2048 * ic + 128 * tt:2048 * ic + 128 * tt + 128],
                                 WB[:, 1024 * ic + 512 * sl:1024 * ic + 512 * (sl + 1)],
                                 start=(ic == 0), stop=False)
                    t.matmul(ps[:, 512 * sl:512 * (sl + 1)], self.onesr_b[:, 0:128],
                             self.bvrow[:, 512 * sl:512 * (sl + 1)],
                             start=False, stop=True)
                s_vp.inc(t.last)
        self.on("tensor", f_v)

        def f_ve(s):
            for tt in range(16):
                ps = P1 if tt % 2 == 0 else P2
                s_vp.wait(s, tt + 1)
                if tt > 0:
                    s_vs.wait(s, 16 * tt)
                i = s.activation(vstage, ps[:], AF.Copy)
                s_ve.inc(i)
        self.on("scalar", f_ve)

        def f_vs(sync):
            for tt in range(16):
                s_ve.wait(sync, tt + 1)
                dst = self.kvsl_f[1024 + 64 * tt:1024 + 64 * (tt + 1), :]
                i = sync.dma_start(
                    out=dst.rearrange("a n -> (a n)").rearrange("(p f) -> p f", f=1024),
                    in_=vstage)
                s_vs.inc(i, 16)
        self.on("sync", f_vs)

        s_wk = self.sem("wk", "dmas")

        def f_wk(g):
            s_qp.wait(g, NFC)
            i = g.dma_start(out=WA.rearrange("p (c n) -> p c n", n=1024),
                            in_=self.fWk.rearrange("(c p) n -> p c n", p=128))
            s_wk.inc(i, 16)
        self.on("gpsimd", f_wk)

        kstage = Fb[:, 1024:3072]
        ksq = Fb[:, 6144:8192]
        s_kp = self.sem("kp")
        s_ke = self.sem("ke")
        s_ks = self.sem("ks", "dmah")
        s_kn = self.sem("kn")

        def f_k(t):
            s_wk.wait(t, 16)
            s_ve.wait(t, 16)
            for oc in range(NFC):
                if oc > 0:
                    s_ke.wait(t, 2 * oc)
                for sl in range(4):
                    for ic in range(NFC):
                        t.matmul(P0[:, 512 * sl:512 * (sl + 1)],
                                 WA[:, 1024 * ic + 128 * oc:1024 * ic + 128 * oc + 128],
                                 Bb[:, 2048 * ic + 512 * sl:2048 * ic + 512 * (sl + 1)],
                                 start=(ic == 0), stop=(ic == NFC - 1))
                s_kp.inc(t.last)
                s_ke.wait(t, 2 * oc + 2)
                for qh in range(2):
                    ps = P1 if qh == 0 else P2
                    for sl in range(2):
                        c0 = 1024 * qh + 512 * sl
                        t.matmul(ps[0:1, 512 * sl:512 * (sl + 1)], self.ones_b[:],
                                 ksq[:, c0:c0 + 512],
                                 start=(oc == 0), stop=(oc == NFC - 1),
                                 skip_group_check=True)
                s_kp.inc(t.last)
        self.on("tensor", f_k)

        def f_ke(s):
            for oc in range(NFC):
                s_kp.wait(s, 2 * oc + 1)
                if oc > 0:
                    s_ks.wait(s, 64 * oc)
                i = s.activation(kstage, P0[:], AF.Identity, bias=self.bk_c[:, oc:oc + 1])
                s_ke.inc(i)
                i = s.activation(ksq, kstage, AF.Square)
                s_ke.inc(i)
        self.on("scalar", f_ke)

        def f_ks(sync):
            for oc in range(NFC):
                s_ke.wait(sync, 2 * oc + 1)
                for q in range(4):
                    i = sync.dma_start(
                        out=self.kvslK[1024 * q + 128 * oc:1024 * q + 128 * (oc + 1), :],
                        in_=kstage[:, 512 * q:512 * (q + 1)])
                    s_ks.inc(i, 16)
        self.on("sync", f_ks)

        def f_kn(v):
            s_kp.wait(v, 2 * NFC)
            v.tensor_copy(self.negC[:, 0:1024], P1[0:1, :])
            i = v.tensor_copy(self.negC[:, 1024:2048], P2[0:1, :])
            s_kn.inc(i)
            s_kn.wait(v, 1)
            i = v.reduce_max(self.Mrow[:, 0:1], self.negC[:], axis=AX.X)
            s_kn.inc(i)
        self.on("vector", f_kn)

        self.scope("gather")
        s_cc = self.sem("cc", "cc")
        ks_n = s_ks.n - s_ks.base

        def f_cc(g):
            for q in range(4):
                s_vs.wait(g, 16 * (4 * q + 4))
                i = g.collective_compute("AllGather", ALU.bypass,
                                         replica_groups=[list(range(NCORES))],
                                         ins=[self.kvsl_f[1024 + 256 * q:1280 + 256 * q, :]],
                                         outs=[self.kvgV[q][:]])
                s_cc.inc(i)
            for q in range(4):
                if q == 0:
                    s_ks.wait(g, ks_n)
                i = g.collective_compute("AllGather", ALU.bypass,
                                         replica_groups=[list(range(NCORES))],
                                         ins=[self.kvslK[1024 * q:1024 * (q + 1), :]],
                                         outs=[self.kvgK[q][:]])
                s_cc.inc(i)
        self.on("gpsimd", f_cc)

        qn = self.Drows[:, 0:2048]
        s_qn = self.sem("qn")
        s_qs = self.sem("qs")

        def f_qsq(s):
            s_kp.wait(s, 2 * NFC)
            s_ks.wait(s, 16 * NFC)
            for fc in range(NFC):
                if fc > 0:
                    s_qn.wait(s, fc)
                i = s.activation(ksq, Cb[:, 2048 * fc:2048 * (fc + 1)], AF.Square)
                s_qs.inc(i)
        self.on("scalar", f_qsq)

        def f_qnt(t):
            s_kn.wait(t, 1)
            for fc in range(NFC):
                s_qs.wait(t, fc + 1)
                for qh in range(2):
                    ps = P1 if qh == 0 else P2
                    for sl in range(2):
                        c0 = 1024 * qh + 512 * sl
                        t.matmul(ps[0:1, 512 * sl:512 * (sl + 1)], self.ones_b[:],
                                 ksq[:, c0:c0 + 512],
                                 start=(fc == 0), stop=(fc == NFC - 1),
                                 skip_group_check=True)
                s_qn.inc(t.last)
        self.on("tensor", f_qnt)
        qn_pe = s_qn.n - s_qn.base

        def f_qnv(v):
            s_qn.wait(v, qn_pe)
            v.tensor_copy(qn[:, 0:1024], P1[0:1, :])
            i = v.tensor_copy(qn[:, 1024:2048], P2[0:1, :])
            s_qn.inc(i)
        self.on("vector", f_qnv)

        s_m = self.sem("m")

        def f_mv2(v):
            s_kn.wait(v, 2)
            s_qn.wait(v, qn_pe + 1)
            i = v.tensor_scalar(qn, qn, self.Mrow[:, 0:1], None, ALU.mult)
            s_m.inc(i)
        self.on("vector", f_mv2)

        def f_ma(s):
            s_m.wait(s, 1)
            i = s.activation(qn, qn, AF.Sqrt)
            s_m.inc(i)
        self.on("scalar", f_ma)

        def f_mv3(v):
            s_m.wait(v, 2)
            i = v.tensor_scalar(self.negC[:], qn, -1.0, None, ALU.mult)
            s_m.inc(i)
        self.on("vector", f_mv3)
        self.done(s_m); self.done(s_qe)
        self.bar()
        self.dump_sbuf(f"dbg_negC{li}", self.negC[:], [1, TOK], BF16)
        self.dump_sbuf(f"dbg_q{li}", Cb[:], [128, 8 * TOK], BF16)

        self.scope("fattn")
        KD = [Db[:, 0:4096], Db[:, 4096:8192]]
        VD = [Db[:, 8192:12288], Db[:, 12288:16384]]
        s_kv = self.sem("kv", "dmah")
        s_sc = self.sem("sc")
        s_ex = self.sem("ex")
        s_avp = self.sem("avp")
        s_avd = self.sem("avd")

        def f_kvs(sync):
            for qr in range(32):
                q, blk, par = qr // 8, qr % 8, qr % 2
                if qr >= 2:
                    s_avp.wait(sync, 32 * (qr - 1))
                if blk == 0:
                    s_cc.wait(sync, 5 + q)
                for fc in range(NFC):
                    i = sync.dma_start(
                        out=KD[par].rearrange("p (c n) -> p c n", n=512)[:, fc, :],
                        in_=self.kvgK[q][1024 * blk + 128 * fc:
                                         1024 * blk + 128 * (fc + 1), :])
                    s_kv.inc(i, 16)
                for ktl in range(4):
                    src = self.kvgV[q][256 * blk + 64 * ktl:256 * blk + 64 * (ktl + 1), :]
                    i = sync.dma_start(
                        out=VD[par].rearrange("p (c n) -> p c n", n=1024)[:, ktl, :],
                        in_=src.rearrange("a n -> (a n)").rearrange("(p f) -> p f", f=1024))
                    s_kv.inc(i, 16)
        self.on("sync", f_kvs)

        def sums_mm(t, qr, ktl, qh):
            for qt in range(8):
                e0 = 2048 * ktl + 1024 * qh + 128 * qt
                col = 8 * qh + qt
                first = (qr == 0 and ktl == 0 and qh == 0 and qt == 0)
                lastm = (qr == 31 and ktl == 3 and qh == 1 and qt == 7)
                t.matmul(P2[:, col:col + 1], Eb[:, e0:e0 + 128], self.ones_b[:],
                         start=first, stop=lastm, skip_group_check=True)

        def f_att(t):
            for qr in range(32):
                par = qr % 2
                s_kv.wait(t, 192 * qr + 128)
                for i8 in range(8):
                    ktl, qh = i8 // 2, i8 % 2
                    g = 8 * qr + i8
                    half = P0[:, 0:1024] if i8 % 2 == 0 else P0[:, 1024:2048]
                    if g >= 2:
                        s_ex.wait(t, g - 1)
                    for fc in range(NFC):
                        for sl in range(2):
                            c0 = 2048 * fc + 1024 * qh + 512 * sl
                            t.matmul(half[:, 512 * sl:512 * (sl + 1)],
                                     KD[par].rearrange("p (c n) -> p c n", n=512)[:, fc, 128 * ktl:128 * (ktl + 1)],
                                     Cb[:, c0:c0 + 512],
                                     start=(fc == 0), stop=False)
                    for sl in range(2):
                        t.matmul(half[:, 512 * sl:512 * (sl + 1)],
                                 self.onesr_b[:, 0:128],
                                 self.negC[:, 1024 * qh + 512 * sl:1024 * qh + 512 * (sl + 1)],
                                 start=False, stop=True)
                    s_sc.inc(t.last)
                    if i8 >= 1:
                        s_ex.wait(t, g)
                        sums_mm(t, qr, (i8 - 1) // 2, (i8 - 1) % 2)
                s_ex.wait(t, 8 * (qr + 1))
                sums_mm(t, qr, 3, 1)
                s_kv.wait(t, 192 * (qr + 1))
                for fc in range(NFC):
                    for qh in range(2):
                        for sl in range(2):
                            gidx = 32 * qr + 4 * fc + 2 * qh + sl
                            bank = 512 * (gidx % 2)
                            if gidx >= 2:
                                s_avd.wait(t, gidx - 1)
                            for ktl in range(4):
                                e0 = 2048 * ktl + 1024 * qh + 512 * sl
                                t.matmul(P1[:, bank:bank + 512],
                                         VD[par].rearrange("p (c n) -> p c n", n=1024)[:, ktl, 128 * fc:128 * (fc + 1)],
                                         Eb[:, e0:e0 + 512],
                                         start=(ktl == 0), stop=(ktl == 3))
                            s_avp.inc(t.last)
        self.on("tensor", f_att)

        def f_exp(s):
            for qr in range(32):
                for i8 in range(8):
                    ktl, qh = i8 // 2, i8 % 2
                    g = 8 * qr + i8
                    half = P0[:, 0:1024] if i8 % 2 == 0 else P0[:, 1024:2048]
                    s_sc.wait(s, g + 1)
                    if qr >= 1:
                        s_avp.wait(s, 32 * qr)
                    e0 = 2048 * ktl + 1024 * qh
                    i = s.activation(Eb[:, e0:e0 + 1024], half, AF.Exp, scale=SCL_F)
                    s_ex.inc(i)
        self.on("scalar", f_exp)

        def f_avd(v):
            for qr in range(32):
                for fc in range(NFC):
                    for qh in range(2):
                        for sl in range(2):
                            gidx = 32 * qr + 4 * fc + 2 * qh + sl
                            bank = 512 * (gidx % 2)
                            s_avp.wait(v, gidx + 1)
                            c0 = 2048 * fc + 1024 * qh + 512 * sl
                            dst = A[:, c0:c0 + 512]
                            if qr == 0:
                                i = v.tensor_copy(dst, P1[:, bank:bank + 512])
                            else:
                                i = v.tensor_add(dst, dst, P1[:, bank:bank + 512])
                            s_avd.inc(i)
        self.on("vector", f_avd)
        self.done(s_avp); self.done(s_avd)
        self.bar()
        self.dump_sbuf(f"dbg_av{li}", A[:], [128, 8 * TOK], F32)

        self.scope("norm")
        s_nr = self.sem("nr")
        s_nrD = self.sem("nrD", "dmah")

        def f_rc(v):
            i = v.reciprocal(Ff[:, 0:16], P2[:, 0:16])
            s_nr.inc(i)
        self.on("vector", f_rc)

        def f_rt(t):
            s_nr.wait(t, 1)
            i = t.transpose(P1[0:16, 0:128], Ff[:, 0:16], self.eye_f[:])
            s_nr.inc(i)
        self.on("tensor", f_rt)

        def f_rv(v):
            s_nr.wait(v, 2)
            i = v.tensor_copy(self.tr16[:], P1[0:16, 0:128])
            s_nr.inc(i)
        self.on("vector", f_rv)

        def f_rb(sync):
            s_nr.wait(sync, 3)
            i = sync.dma_start(out=self.rowb[:], in_=self.tr16[:])
            s_nrD.inc(i, 16)
            s_nrD.wait(sync, 16)
            i = sync.dma_start(out=self.Drows[:, 0:2048],
                               in_=self.rowb.rearrange("a b -> (a b)")[None, :])
            s_nrD.inc(i, 16)
        self.on("sync", f_rb)

        def f_rbc(t):
            s_nrD.wait(t, 32)
            for sl in range(4):
                t.matmul(P0[:, 512 * sl:512 * (sl + 1)], self.onesr_f[:],
                         self.Drows[:, 512 * sl:512 * (sl + 1)], start=True, stop=True)
            s_nr.inc(t.last)
        self.on("tensor", f_rbc)

        def f_nm(v):
            s_nr.wait(v, 4)
            for fc in range(NFC):
                i = v.tensor_mul(Bb[:, 2048 * fc:2048 * (fc + 1)],
                                 A[:, 2048 * fc:2048 * (fc + 1)], P0[:])
            s_nr.inc(i)
        self.on("vector", f_nm)
        nm_n = s_nr.n - s_nr.base

        self.scope("wo")
        s_wo = self.sem("wo", "dmas")

        def f_wodma(g):
            s_avp.wait(g, 1024)
            i = g.dma_start(out=Eb.rearrange("p (c n) -> p c n", n=1024),
                            in_=self.fWo.rearrange("(c p) n -> p c n", p=128))
            s_wo.inc(i, 16)
        self.on("gpsimd", f_wodma)

        s_ur = self.sem("ur", "dmah")

        def f_url(sync):
            s_nr.wait(sync, nm_n)
            for fc in range(NFC):
                i = sync.dma_start(out=A[:, 2048 * fc:2048 * (fc + 1)],
                                   in_=self.spu[:, 2048 * fc:2048 * (fc + 1)])
                s_ur.inc(i, 16)
        self.on("sync", f_url)

        s_u2p = self.sem("u2p")
        s_u2d = self.sem("u2d")

        def f_wo(t):
            s_wo.wait(t, 16)
            s_nr.wait(t, nm_n)
            for oc in range(NFC):
                if oc > 0:
                    s_u2d.wait(t, 2 * oc)
                for sl in range(4):
                    for ic in range(NFC):
                        t.matmul(P0[:, 512 * sl:512 * (sl + 1)],
                                 Eb[:, 1024 * ic + 128 * oc:1024 * ic + 128 * oc + 128],
                                 Bb[:, 2048 * ic + 512 * sl:2048 * ic + 512 * (sl + 1)],
                                 start=(ic == 0), stop=(ic == NFC - 1))
                s_u2p.inc(t.last)
        self.on("tensor", f_wo)

        def f_u2(v):
            s_ur.wait(v, 16 * NFC)
            for oc in range(NFC):
                s_u2p.wait(v, oc + 1)
                uc = A[:, 2048 * oc:2048 * (oc + 1)]
                i = v.scalar_tensor_tensor(uc, P0[:], self.bo_c[:, oc:oc + 1], uc,
                                           ALU.add, ALU.add)
                s_u2d.inc(i)
                i = v.tensor_copy(Db[:, 2048 * oc:2048 * (oc + 1)], uc)
                s_u2d.inc(i)
        self.on("vector", f_u2)
        self.done(s_u2d)
        self.bar()
        self.dump_sbuf(f"dbg_u2_{li}", A[:], [128, 8 * TOK], F32)

        self.scope("gates")
        s_gt = self.sem("gt")
        s_gr = self.sem("gr")

        def f_gsc(t):
            for sl in range(4):
                for ic in range(NFC):
                    t.matmul(P0[0:8, 512 * sl:512 * (sl + 1)],
                             self.wg_sb[:, 8 * ic:8 * (ic + 1)],
                             Db[:, 2048 * ic + 512 * sl:2048 * ic + 512 * (sl + 1)],
                             start=(ic == 0), stop=False)
                t.matmul(P0[0:8, 512 * sl:512 * (sl + 1)], self.bgrow[:],
                         self.onesr_b[:], start=False, stop=True)
            s_gt.inc(t.last)
        self.on("tensor", f_gsc)

        def f_gev(v):
            s_gt.wait(v, 1)
            i = v.tensor_copy(Ff[0:8, 0:2048], P0[0:8, :])
            s_gt.inc(i)
        self.on("vector", f_gev)

        def f_gtr(t):
            s_gt.wait(t, 2)
            for c in range(16):
                ps = P1 if c % 2 == 0 else P2
                if c >= 2:
                    s_gr.wait(t, c - 1)
                i = t.transpose(ps[:, 0:8], Ff[0:8, 128 * c:128 * (c + 1)],
                                self.eye_f[0:8, 0:8])
                s_gt.inc(i)
        self.on("tensor", f_gtr)

        def f_gcp(v):
            for c in range(16):
                ps = P1 if c % 2 == 0 else P2
                s_gt.wait(v, 3 + c)
                i = v.tensor_copy(self.gTM[:, 8 * c:8 * (c + 1)], ps[:, 0:8])
                s_gr.inc(i)
        self.on("vector", f_gcp)

        s_gs = self.sem("gs")
        nm_col = Ff[:, 3072:3073]
        sm_col = Ff[:, 3073:3074]

        def f_gsm(v):
            for c in range(16):
                g = self.gTM[:, 8 * c:8 * (c + 1)]
                s_gr.wait(v, c + 1)
                if c > 0:
                    s_gs.wait(v, 2 * c)
                i = v.reduce_max(nm_col, g, axis=AX.X, negate=True)
                s_gs.inc(i)
        self.on("vector", f_gsm)

        def f_gse(s):
            for c in range(16):
                g = self.gTM[:, 8 * c:8 * (c + 1)]
                s_gs.wait(s, 2 * c + 1)
                i = s.activation(g, g, AF.Exp, bias=nm_col)
                s_gs.inc(i)
        self.on("scalar", f_gse)

        s_gf = self.sem("gf")

        s_gs2 = self.sem("gs2")

        def f_gsm2(v):
            for c in range(16):
                g = self.gTM[:, 8 * c:8 * (c + 1)]
                s_gs.wait(v, 2 * c + 2)
                i = v.reduce_sum(sm_col, g, axis=AX.X)
                s_gs2.inc(i)
                s_gs2.wait(v, 3 * c + 1)
                i = v.reciprocal(sm_col, sm_col)
                s_gs2.inc(i)
                s_gs2.wait(v, 3 * c + 2)
                i = v.tensor_scalar(g, g, sm_col, None, ALU.mult)
                s_gs2.inc(i)
                s_gf.inc(v.engine_nop()) if False else None
                s_gf.inc(i) if False else None
        self.on("vector", f_gsm2)
        s_gf.n = s_gf.base

        s_gb = self.sem("gb")
        s_gbc = self.sem("gbc")

        def f_gbt(t):
            for c in range(16):
                ps = P1 if c % 2 == 0 else P2
                s_gs2.wait(t, 3 * c + 3)
                if c >= 2:
                    s_gbc.wait(t, c - 1)
                i = t.transpose(ps[0:8, 0:128], self.gTM[:, 8 * c:8 * (c + 1)],
                                self.eye_f[:])
                s_gb.inc(i)
        self.on("tensor", f_gbt)

        def f_gbc(v):
            for c in range(16):
                ps = P1 if c % 2 == 0 else P2
                s_gb.wait(v, c + 1)
                i = v.tensor_copy(self.Fb[0:8, 4096 + 128 * c:4096 + 128 * (c + 1)],
                                  ps[0:8, 0:128])
                s_gbc.inc(i)
        self.on("vector", f_gbc)
        self.done(s_gbc); self.done(s_gs2)
        self.bar()
        self.dump_sbuf(f"dbg_g{li}", self.gTM[:], [128, 128], F32)

        self.scope("xp")
        s_xp = self.sem("xp", "dmah")
        s_xpd = self.sem("xpd")
        xch = Ff[:, 0:2048]

        def f_xp(sync):
            for fc in range(NFC):
                if fc > 0:
                    s_xpd.wait(sync, fc)
                i = sync.dma_start(out=xch, in_=self.spx[:, 2048 * fc:2048 * (fc + 1)])
                s_xp.inc(i, 16)
                s_xpd.wait(sync, fc + 1)
                i = sync.dma_start(out=self.spx[:, 2048 * fc:2048 * (fc + 1)],
                                   in_=A[:, 2048 * fc:2048 * (fc + 1)])
                s_xp.inc(i, 16)
        self.on("sync", f_xp)

        def f_xpd(v):
            for fc in range(NFC):
                s_xp.wait(v, 32 * fc + 16)
                uc = A[:, 2048 * fc:2048 * (fc + 1)]
                i = v.scalar_tensor_tensor(uc, uc, 0.5, xch, ALU.mult, ALU.add)
                s_xpd.inc(i, 1)
        self.on("vector", f_xpd)
        self.done(s_xp); self.done(s_xpd)
        self.bar()

        self.scope("moe")
        self.emit_moe(li, last)

        if self.dbg and not last:
            self.dump_dram(f"dbg_out{li}", self.spx[:], [128, 8 * TOK], F32)
            self.bar()
        if not last:
            s_xr = self.sem("xr", "dmah")

            def f_xr(sync):
                for fc in range(NFC):
                    i = sync.dma_start(out=A[:, 2048 * fc:2048 * (fc + 1)],
                                       in_=self.spx[:, 2048 * fc:2048 * (fc + 1)])
                    s_xr.inc(i, 16)
            self.on("sync", f_xr)
            self.done(s_xr)
            self.bar()
        self.release_scope()

    def emit_moe(self, li, last):
        P0, P1, P2 = self.P0, self.P1, self.P2
        A, Ab, Bf, Cb, Db, Eb, Ff, Fb = (self.A, self.Ab, self.Bf, self.Cb,
                                         self.Db, self.Eb, self.Ff, self.Fb)
        s_w1 = self.sem("w1", "dmas")
        s_w2 = self.sem("w2", "dmas")
        s_hp = self.sem("hp")
        s_he = self.sem("he")
        s_b2 = self.sem("b2", "dmah")
        s_ymm = self.sem("ymm")
        s_yd = self.sem("yd")
        s_p7p = self.sem("p7p")
        s_p7t = self.sem("p7t")
        s_p7d = self.sem("p7d")
        s_p7x = self.sem("p7x", "dmah")
        E1 = [Eb[:, 0:4096], Eb[:, 4096:8192]]
        C1 = [Cb[:, 0:8192], Cb[:, 8192:16384]]
        b2row = Fb[0:1, 6144:7168]
        gFM = Fb[0:8, 4096:6144]

        def f_w12(g):
            def w1_eighth(te, ei):
                e = te % NE
                gi = 8 * te + ei
                if gi >= 2:
                    s_hp.wait(g, 4 * (gi - 1))
                srcw = self.fW1[e].rearrange("(c p) d -> p c d", p=128)
                i = g.dma_start(out=E1[gi % 2].rearrange("p (c n) -> p c n", n=512),
                                in_=srcw[:, :, 512 * ei:512 * (ei + 1)])
                s_w1.inc(i, 16)

            for ei in range(8):
                w1_eighth(0, ei)
            for tb in range(2):
                for e in range(NE):
                    te = 8 * tb + e
                    for fq in range(4):
                        gq = 4 * te + fq
                        if gq >= 2:
                            s_ymm.wait(g, 8 * (gq - 1))
                        srcw = self.fW2[e].rearrange("(c p) f -> p c f", p=128)
                        i = g.dma_start(out=C1[gq % 2].rearrange("p (c n) -> p c n", n=256),
                                        in_=srcw[:, :, 256 * fq:256 * (fq + 1)])
                        s_w2.inc(i, 16)
                    if te + 1 < 16:
                        for ei in range(8):
                            w1_eighth(te + 1, ei)
        self.on("gpsimd", f_w12)



        gFM_t = self.Fb[0:8, 4096:6144]

        def p7_pe(t, tb):
            s_yd.wait(t, 256 * (tb + 1))
            for fc in range(NFC):
                j = 8 * tb + fc
                if j > 0:
                    s_p7d.wait(t, 9 * j)
                for sl in range(2):
                    t.matmul(P0[:, 512 * sl:512 * (sl + 1)],
                             self.b2sb[:, 128 * fc:128 * (fc + 1)],
                             gFM_t[:, 1024 * tb + 512 * sl:1024 * tb + 512 * (sl + 1)],
                             start=True, stop=True)
                s_p7p.inc(t.last)
                for tt in range(8):
                    ps = P1 if tt % 2 == 0 else P2
                    if tt >= 2:
                        s_p7d.wait(t, 9 * j + tt)
                    i = t.transpose(ps[:, 0:128],
                                    self.Bf[:, 1024 * tt + 128 * fc:1024 * tt + 128 * (fc + 1)],
                                    self.eye_f[:])
                    s_p7t.inc(i)

        def p7_dve(v, tb):
            for fc in range(NFC):
                j = 8 * tb + fc
                fp = Ff[:, 1024 * (fc % 2):1024 * (fc % 2 + 1)]
                s_p7p.wait(v, j + 1)
                s_p7x.wait(v, 16 * (j + 1))
                i = v.scalar_tensor_tensor(fp, P0[:, 0:1024], 0.5, fp, ALU.mult, ALU.add)
                s_p7d.inc(i)
                for tt in range(8):
                    ps = P1 if tt % 2 == 0 else P2
                    s_p7t.wait(v, 8 * j + tt + 1)
                    sl = fp[:, 128 * tt:128 * (tt + 1)]
                    i = v.scalar_tensor_tensor(sl, ps[:, 0:128], 0.5, sl, ALU.mult, ALU.add)
                    s_p7d.inc(i)

        def f_hmm(t):
            for tb in range(2):
                for e in range(NE):
                    te = 8 * tb + e
                    if tb == 1 and e == 0:
                        s_p7d.wait(t, 72)
                    for dc in range(32):
                        gd = 32 * te + dc
                        ei = dc // 4
                        gi = 8 * te + ei
                        s_w1.wait(t, 16 * (gi + 1))
                        if gd >= 2:
                            s_he.wait(t, gd - 1)
                        half = P0[:, 0:1024] if dc % 2 == 0 else P0[:, 1024:2048]
                        for sl in range(2):
                            for ic in range(NFC):
                                t.matmul(half[:, 512 * sl:512 * (sl + 1)],
                                         E1[gi % 2][:, 512 * ic + 128 * (dc % 4):512 * ic + 128 * (dc % 4) + 128],
                                         Db[:, 2048 * ic + 1024 * tb + 512 * sl:2048 * ic + 1024 * tb + 512 * (sl + 1)],
                                         start=(ic == 0), stop=(ic == NFC - 1))
                        s_hp.inc(t.last)
                    s_he.wait(t, 32 * te + 32)
                    s_b2.wait(t, 16 * (te + 1))
                    for fq in range(4):
                        gq = 4 * te + fq
                        s_w2.wait(t, 16 * (gq + 1))
                        for tt in range(8):
                            gy = 8 * gq + tt
                            ps = P1 if tt % 2 == 0 else P2
                            if gy >= 2:
                                s_yd.wait(t, gy - 1)
                            for dc in range(32):
                                t.matmul(ps[:, 0:256],
                                         Ab[:, 1024 * dc + 128 * tt:1024 * dc + 128 * (tt + 1)],
                                         C1[gq % 2][:, 256 * dc:256 * (dc + 1)],
                                         start=(dc == 0), stop=False)
                            t.matmul(ps[:, 0:256], self.onesr_b[:, 0:128],
                                     b2row[:, 256 * fq:256 * (fq + 1)],
                                     start=False, stop=True)
                            s_ymm.inc(t.last)
                p7_pe(t, tb)
        self.on("tensor", f_hmm)

        def f_he(s):
            for tb in range(2):
                for e in range(NE):
                    te = 8 * tb + e
                    for dc in range(32):
                        gd = 32 * te + dc
                        s_hp.wait(s, gd + 1)
                        if te > 0 and dc == 0:
                            s_ymm.wait(s, 32 * te)
                        half = P0[:, 0:1024] if dc % 2 == 0 else P0[:, 1024:2048]
                        i = s.activation(Ab[:, 1024 * dc:1024 * (dc + 1)], half,
                                         AF.Relu, bias=self.b1T[:, 32 * e + dc:32 * e + dc + 1])
                        s_he.inc(i)
        self.on("scalar", f_he)

        def f_yd(v):
            for tb in range(2):
                if tb == 1:
                    s_p7d.wait(v, 72)
                for e in range(NE):
                    te = 8 * tb + e
                    for fq in range(4):
                        gq = 4 * te + fq
                        for tt in range(8):
                            gy = 8 * gq + tt
                            ps = P1 if tt % 2 == 0 else P2
                            s_ymm.wait(v, gy + 1)
                            dst = Bf[:, 1024 * tt + 256 * fq:1024 * tt + 256 * (fq + 1)]
                            c0 = 8 * (8 * tb + tt) + e
                            gcol = self.gTM[:, c0:c0 + 1]
                            if e == 0:
                                i = v.tensor_scalar(dst, ps[:, 0:256], gcol, None, ALU.mult)
                            else:
                                i = v.scalar_tensor_tensor(dst, ps[:, 0:256], gcol, dst,
                                                           ALU.mult, ALU.add)
                            s_yd.inc(i)
                p7_dve(v, tb)
        self.on("vector", f_yd)

        st0 = self.s_st.n

        def f_p7sync(sync):
            for tb in range(2):
                for e in range(NE):
                    te = 8 * tb + e
                    if te > 0:
                        s_ymm.wait(sync, 32 * te)
                    i = sync.dma_start(out=b2row, in_=self.b2sb[e:e + 1, :])
                    s_b2.inc(i, 16)
                for fc in range(NFC):
                    j = 8 * tb + fc
                    if j >= 2:
                        self.s_st.wait_abs(sync, st0 + 16 * (j - 1))
                    i = sync.dma_start(out=Ff[:, 1024 * (fc % 2):1024 * (fc % 2 + 1)],
                                       in_=self.spx[:, 2048 * fc + 1024 * tb:2048 * fc + 1024 * tb + 1024])
                    s_p7x.inc(i, 16)
                    s_p7d.wait(sync, 9 * j + 9)
                    if last:
                        dst = self.yT.rearrange("(c p) t -> p c t", p=128)[:, fc, 1024 * tb:1024 * tb + 1024]
                    else:
                        dst = self.spx[:, 2048 * fc + 1024 * tb:2048 * fc + 1024 * tb + 1024]
                    i = sync.dma_start(out=dst, in_=Ff[:, 1024 * (fc % 2):1024 * (fc % 2 + 1)])
                    self.s_st.inc(i, 16)
        self.on("sync", f_p7sync)
        s_p7d.n = s_p7d.base + 144

        self.done(self.s_st); self.done(s_p7d); self.done(s_yd); self.done(s_he)
        self.done(s_w1); self.done(s_w2); self.done(s_b2); self.done(s_p7x)
        self.bar()


    def emit_attn(self, last):
        P0, P1, P2 = self.P0, self.P1, self.P2
        A, Bb, B65, Cb, C65 = self.A, self.Bb, self.B65, self.Cb, self.C65
        Db, Df, Eb, Ff, Fb = self.Db, self.Df, self.Eb, self.Ff, self.Fb

        self.scope("aln")
        s_lnp = self.emit_ln("double")
        lnp_n = s_lnp.n - s_lnp.base
        stage = Ff[:, 2048:4096]
        s_app = self.sem("app")
        s_spl = self.sem("spl", "dmah")

        def f_app(v):
            s_lnp.wait(v, lnp_n)
            for fc in range(NFC):
                xc = A[:, 2048 * fc:2048 * (fc + 1)]
                if fc > 0:
                    s_spl.wait(v, 16 * fc)
                v.tensor_sub(stage, xc, P0[:])
                s_app.inc(v.last)
                for qh in range(2):
                    ps = P1 if qh == 0 else P2
                    v.tensor_mul(stage[:, 1024 * qh:1024 * (qh + 1)],
                                 stage[:, 1024 * qh:1024 * (qh + 1)], ps[:])
                i = v.tensor_scalar(Bb[:, 2048 * fc:2048 * (fc + 1)], stage,
                                    self.anw_c[:, fc:fc + 1], self.anb_c[:, fc:fc + 1],
                                    ALU.mult, ALU.add)
                s_app.inc(i)
        self.on("vector", f_app)

        def f_spl(sync):
            for fc in range(NFC):
                s_app.wait(sync, 2 * fc + 1)
                s_spl.inc(sync.dma_start(out=self.spx[:, 2048 * fc:2048 * (fc + 1)],
                                         in_=A[:, 2048 * fc:2048 * (fc + 1)]), 16)
        self.on("sync", f_spl)
        self.done(s_app); self.done(s_spl)
        self.bar()

        if self.attn_trunc(1):
            return
        s_wl = self.sem("wl", "dmas")

        def f_wdma(g):
            i = g.dma_start(out=Eb[:, 0:4096].rearrange("p (c n) -> p c n", n=512),
                            in_=self.aWq.rearrange("(c p) n -> p c n", p=128))
            s_wl.inc(i, 16)
            i = g.dma_start(out=Eb[:, 4096:5120].rearrange("p (c n) -> p c n", n=128),
                            in_=self.aWkv.rearrange("(c p) n -> p c n", p=128))
            s_wl.inc(i, 16)
        self.on("gpsimd", f_wdma)

        self.scope("aproj")
        s_qp = self.sem("qp")
        s_qe = self.sem("qe")

        def f_q(t):
            s_wl.wait(t, 16)
            for h in range(8):
                if h > 0:
                    s_qe.wait(t, h)
                for sl in range(4):
                    for ic in range(NFC):
                        t.matmul(P0[0:64, 512 * sl:512 * (sl + 1)],
                                 Eb[:, 512 * ic + 64 * h:512 * ic + 64 * h + 64],
                                 Bb[:, 2048 * ic + 512 * sl:2048 * ic + 512 * (sl + 1)],
                                 start=(ic == 0), stop=(ic == NFC - 1))
                s_qp.inc(t.last)
        self.on("tensor", f_q)

        def f_qe(s):
            for h in range(8):
                s_qp.wait(s, h + 1)
                i = s.activation(C65[0:64, 2048 * h:2048 * h + 2048],
                                 P0[0:64, :], AF.Copy)
                s_qe.inc(i)
        self.on("scalar", f_qe)

        s_kp = self.sem("kp")
        s_ke = self.sem("ke")
        s_ks = self.sem("ks", "dmah")
        s_kn = self.sem("kn")
        kstage = Fb[0:64, 0:2048]
        ksq = Fb[0:64, 2048:4096]
        vf32 = Ff[0:64, 2048:4096]

        def f_kv(t):
            s_wl.wait(t, 32)
            s_qe.wait(t, 8)
            for sl in range(4):
                for ic in range(NFC):
                    t.matmul(P0[0:64, 512 * sl:512 * (sl + 1)],
                             Eb[:, 4096 + 128 * ic:4096 + 128 * ic + 64],
                             Bb[:, 2048 * ic + 512 * sl:2048 * ic + 512 * (sl + 1)],
                             start=(ic == 0), stop=(ic == NFC - 1))
            s_kp.inc(t.last)
            for sl in range(4):
                ps = P1 if sl < 2 else P2
                for ic in range(NFC):
                    t.matmul(ps[0:64, 512 * (sl % 2):512 * (sl % 2 + 1)],
                             Eb[:, 4096 + 128 * ic + 64:4096 + 128 * ic + 128],
                             Bb[:, 2048 * ic + 512 * sl:2048 * ic + 512 * (sl + 1)],
                             start=(ic == 0), stop=(ic == NFC - 1))
            s_kp.inc(t.last)
        self.on("tensor", f_kv)

        def f_kve(s):
            s_kp.wait(s, 1)
            i = s.activation(kstage, P0[0:64, :], AF.Copy)
            s_ke.inc(i)
            i = s.activation(ksq, kstage, AF.Square)
            s_ke.inc(i)
        self.on("scalar", f_kve)

        def f_kvv(v):
            s_kp.wait(v, 2)
            v.tensor_copy(vf32[:, 0:1024], P1[0:64, :])
            i = v.tensor_copy(vf32[:, 1024:2048], P2[0:64, :])
            s_ke.inc(i)
        self.on("vector", f_kvv)

        def f_ksdma(sync):
            s_ke.wait(sync, 1)
            i = sync.dma_start(out=self.kvsl_a[0:64, :], in_=kstage)
            s_ks.inc(i, 16)
        self.on("sync", f_ksdma)

        def f_knt(t):
            s_ke.wait(t, 3)
            for qh in range(2):
                ps = P1 if qh == 0 else P2
                for sl in range(2):
                    c0 = 1024 * qh + 512 * sl
                    t.matmul(ps[0:1, 512 * sl:512 * (sl + 1)], self.ones_b[0:64, :],
                             ksq[:, c0:c0 + 512], start=True, stop=True,
                             skip_group_check=True)
            s_kn.inc(t.last)
        self.on("tensor", f_knt)

        def f_knv(v):
            s_kn.wait(v, 1)
            v.tensor_copy(self.negC[:, 0:1024], P1[0:1, :])
            i = v.tensor_copy(self.negC[:, 1024:2048], P2[0:1, :])
            s_kn.inc(i)
            s_kn.wait(v, 2)
            i = v.reduce_max(self.Mrow[:, 0:1], self.negC[:], axis=AX.X)
            s_kn.inc(i)
        self.on("vector", f_knv)

        s_vtt = self.sem("vtt")
        s_vt = self.sem("vt")
        s_vts = self.sem("vts", "dmah")
        vtst = Fb[:, 2048:2176]

        def f_vt(t):
            s_kn.wait(t, 2)
            for tt in range(16):
                ps = P1 if tt % 2 == 0 else P2
                if tt >= 2:
                    s_vt.wait(t, tt - 1)
                i = t.transpose(ps[:, 0:64], vf32[:, 128 * tt:128 * (tt + 1)],
                                self.eye_f[0:64, 0:64])
                s_vtt.inc(i)
        self.on("tensor", f_vt)

        def f_vte(v):
            for tt in range(16):
                ps = P1 if tt % 2 == 0 else P2
                s_vtt.wait(v, tt + 1)
                if tt >= 2:
                    s_vts.wait(v, 16 * (tt - 1))
                i = v.tensor_copy(vtst[:, 64 * (tt % 2):64 * (tt % 2) + 64], ps[:, 0:64])
                s_vt.inc(i)
        self.on("vector", f_vte)

        def f_vts(sync):
            for tt in range(16):
                s_vt.wait(sync, tt + 1)
                dst = self.kvsl_a[64 + 4 * tt:64 + 4 * (tt + 1), :]
                i = sync.dma_start(
                    out=dst.rearrange("a n -> (a n)").rearrange("(p f) -> p f", f=64),
                    in_=vtst[:, 64 * (tt % 2):64 * (tt % 2) + 64])
                s_vts.inc(i, 16)
        self.on("sync", f_vts)
        if self.dbg:
            self.done(s_vts); self.done(s_qe); self.done(s_ke)
            self.bar()
            self.dump_sbuf("dbg_axn", Bb[:], [128, 8 * TOK], BF16)

        if self.attn_trunc(2):
            return
        s_cc = self.sem("cc", "cc")
        ks_n, vts_n = s_ks.n - s_ks.base, 16 * 16

        def f_cc(g):
            s_ks.wait(g, ks_n)
            s_vts.wait(g, vts_n)
            i = g.collective_compute("AllGather", ALU.bypass,
                                     replica_groups=[list(range(NCORES))],
                                     ins=[self.kvsl_a[:]], outs=[self.kvg_a[:]])
            s_cc.inc(i)
        self.on("gpsimd", f_cc)
        cc_n = s_cc.n - s_cc.base

        s_kl = self.sem("kl", "dmah")

        def f_kl(sync):
            s_cc.wait(sync, cc_n)
            for r in range(NCORES):
                i = sync.dma_start(out=B65[0:64, 2048 * r:2048 * (r + 1)],
                                   in_=self.kvg_a[KV_ROWS_A * r:KV_ROWS_A * r + 64, :])
                s_kl.inc(i, 16)
            for kt in range(128):
                r, ktl = kt // 16, kt % 16
                vr0 = KV_ROWS_A * r + 64 + 4 * ktl
                src = self.kvg_a[vr0:vr0 + 4, :]
                i = sync.dma_start(
                    out=Db[:, 65 * kt:65 * kt + 64],
                    in_=src.rearrange("a n -> (a n)").rearrange("(p f) -> p f", f=64))
                s_kl.inc(i, 16)
        self.on("sync", f_kl)
        kl_n = s_kl.n - s_kl.base

        s_kf = self.sem("kf")

        def f_kfill(g):
            s_cc.wait(g, cc_n)
            g.memset(B65[64:65, :], 1.0)
            i = g.memset(Db[:, 0:8320].rearrange("p (t c) -> p t c", c=65)[:, :, 64:65], 1.0)
            s_kf.inc(i, 1)
        self.on("gpsimd", f_kfill)

        s_m = self.sem("m")
        self.done(s_m); self.done(s_kl); self.done(s_kf); self.done(s_qe)
        self.bar()
        if self.attn_trunc(3):
            return
        self.dump_dram("dbg_akvg", self.kvg_a[:], [NCORES * KV_ROWS_A, TOK], BF16)

        self.scope("aqn")
        qsq = Fb[:, 0:2048]
        s_sq2 = self.sem("sq2")
        s_qnm = self.sem("qnm")
        s_ngc = self.sem("ngc")
        s_asc = self.sem("asc")
        s_aex = self.sem("aex")
        s_aav = self.sem("aav")
        s_anm = self.sem("anm")

        def f_hsq(s):
            for h in range(8):
                if h > 0:
                    s_qnm.wait(s, h)
                i = s.activation(qsq[0:64, :], C65[0:64, 2048 * h:2048 * (h + 1)],
                                 AF.Square)
                s_sq2.inc(i)
                s_ngc.wait(s, 4 * h + 2)
                i = s.activation(self.negC[:], self.negC[:], AF.Sqrt)
                s_ngc.inc(i)
        self.on("scalar", f_hsq)

        def f_qnt(t):
            for h in range(8):
                s_sq2.wait(t, h + 1)
                if h > 0:
                    s_ngc.wait(t, 4 * h - 3)
                for qh in range(2):
                    ps = P1 if qh == 0 else P2
                    for sl in range(2):
                        c0 = 1024 * qh + 512 * sl
                        t.matmul(ps[0:1, 512 * sl:512 * (sl + 1)], self.ones_b[0:64, :],
                                 qsq[0:64, c0:c0 + 512], start=True, stop=True,
                                 skip_group_check=True)
                s_qnm.inc(t.last)
        self.on("tensor", f_qnt)

        def f_qnv(v):
            for h in range(8):
                s_qnm.wait(v, h + 1)
                v.tensor_copy(self.negC[:, 0:1024], P1[0:1, :])
                i = v.tensor_copy(self.negC[:, 1024:2048], P2[0:1, :])
                s_ngc.inc(i)
                i = v.tensor_scalar(self.negC[:], self.negC[:], self.Mrow[:, 0:1],
                                    None, ALU.mult)
                s_ngc.inc(i)
                s_ngc.wait(v, 4 * h + 3)
                i = v.tensor_scalar(C65[64:65, 2048 * h:2048 * (h + 1)], self.negC[:],
                                    -1.0, None, ALU.mult)
                s_ngc.inc(i)
        self.on("vector", f_qnv)

        self.done(s_ngc); self.done(s_qnm)
        self.bar()
        if self.attn_trunc(4):
            return
        self.dump_sbuf("dbg_aq", C65[:], [65, 8 * TOK], BF16)

        self.scope("aattn")
        srow_a = Ff[64:65, 0:1024]
        apre = Fb[0:64, 2048:3072]
        apre2 = Fb[0:64, 3072:4096]
        s_adm = self.sem("adm", "dmah")

        def f_att(t):
            T3 = [P0[:, 0:1024], P0[:, 1024:2048], P2[:, 0:1024]]
            for h in range(8):
                hq2 = C65[:, 2048 * h:2048 * (h + 1)]
                for qh in range(2):
                    hq = 2 * h + qh
                    base = 128 * hq

                    def av(t, j, hq=hq, base=base):
                        ep = Eb[:, 1024 * (j % 4):1024 * (j % 4) + 1024]
                        for sl in range(2):
                            t.matmul(P1[0:65, 512 * sl:512 * (sl + 1)],
                                     Db[:, 65 * j:65 * (j + 1)],
                                     ep[:, 512 * sl:512 * (sl + 1)],
                                     start=(j == 0), stop=(j == 127))
                        s_aav.inc(t.last)

                    for kt in range(128):
                        g = base + kt
                        if g >= 3:
                            s_aex.wait(t, g - 2)
                        tile = T3[g % 3]
                        for sl in range(2):
                            t.matmul(tile[:, 512 * sl:512 * (sl + 1)],
                                     B65[:, 128 * kt:128 * (kt + 1)],
                                     hq2[:, 1024 * qh + 512 * sl:1024 * qh + 512 * (sl + 1)],
                                     start=True, stop=True)
                        s_asc.inc(t.last)
                        if kt >= 3:
                            if kt == 3:
                                s_anm.wait(t, 4 * hq)
                            av(t, kt - 3)
                    for j in (125, 126, 127):
                        s_aex.wait(t, base + j + 1)
                        av(t, j)
                    s_anm.wait(t, 4 * hq + 3)
                    for sl in range(2):
                        t.matmul(P1[0:64, 512 * sl:512 * (sl + 1)],
                                 self.ones2d[64:65, :],
                                 srow_a[:, 512 * sl:512 * (sl + 1)],
                                 start=True, stop=True)
                    s_aav.inc(t.last)
        self.on("tensor", f_att)

        def f_exp(s):
            for h in range(8):
                for qh in range(2):
                    hq = 2 * h + qh
                    base = 128 * hq
                    T3 = [P0[:, 0:1024], P0[:, 1024:2048], P2[:, 0:1024]]
                    for kt in range(128):
                        g = base + kt
                        half = T3[g % 3]
                        s_asc.wait(s, g + 1)
                        if kt >= 4:
                            s_aav.wait(s, 129 * hq + kt - 3)
                        elif hq > 0:
                            s_aav.wait(s, 129 * hq - 1)
                        ep = Eb[:, 1024 * (kt % 4):1024 * (kt % 4) + 1024]
                        i = s.activation(ep, half, AF.Exp, scale=SCL_A)
                        s_aex.inc(i)
        self.on("scalar", f_exp)

        def f_nrm(v):
            for h in range(8):
                for qh in range(2):
                    hq = 2 * h + qh
                    s_aav.wait(v, 129 * hq + 128)
                    i = v.tensor_copy(srow_a, P1[64:65, 0:1024])
                    s_anm.inc(i)
                    i = v.tensor_copy(apre, P1[0:64, 0:1024])
                    s_anm.inc(i)
                    i = v.reciprocal(srow_a, srow_a)
                    s_anm.inc(i)
                    s_aav.wait(v, 129 * hq + 129)
                    c = h // 2
                    if h % 2 == 0:
                        dst = Db[0:64, 8320 + 2048 * c + 1024 * qh:8320 + 2048 * c + 1024 * (qh + 1)]
                        i = v.tensor_mul(dst, apre, P1[0:64, 0:1024])
                    else:
                        odd_idx = (hq - 2) // 4 * 2 + qh
                        odd_n = h // 2 * 2 + qh
                        if odd_n > 0:
                            s_adm.wait(v, 16 * odd_n)
                        i = v.tensor_mul(apre2, apre, P1[0:64, 0:1024])
                    s_anm.inc(i)
        self.on("vector", f_nrm)

        def f_adm(sync):
            for h in range(1, 8, 2):
                for qh in range(2):
                    hq = 2 * h + qh
                    c = h // 2
                    s_anm.wait(sync, 4 * hq + 4)
                    dst = Db[64:128, 8320 + 2048 * c + 1024 * qh:8320 + 2048 * c + 1024 * (qh + 1)]
                    i = sync.dma_start(out=dst, in_=apre2)
                    s_adm.inc(i, 16)
        self.on("sync", f_adm)
        self.done(s_anm); self.done(s_adm)
        self.bar()
        if self.attn_trunc(5):
            return
        self.dump_sbuf("dbg_aall", Db[:, 8320:16512], [128, 8192], BF16)

        self.scope("awo")
        s_wo = self.sem("wo", "dmas")

        def f_wodma(g):
            s_kp.wait(g, 2)
            i = g.dma_start(out=Eb[:, 4096:8192].rearrange("p (c n) -> p c n", n=1024),
                            in_=self.aWo.rearrange("(c p) n -> p c n", p=128))
            s_wo.inc(i, 16)
        self.on("gpsimd", f_wodma)

        s_op = self.sem("op")
        s_oe = self.sem("oe")
        s_os = self.sem("os")

        def f_wo(t):
            s_wo.wait(t, 16)
            for oc in range(NFC):
                if oc > 0:
                    s_oe.wait(t, 3 * oc)
                for sl in range(4):
                    for ic in range(4):
                        t.matmul(P0[:, 512 * sl:512 * (sl + 1)],
                                 Eb[:, 4096 + 1024 * ic + 128 * oc:4096 + 1024 * ic + 128 * oc + 128],
                                 Db[:, 8320 + 2048 * ic + 512 * sl:8320 + 2048 * ic + 512 * (sl + 1)],
                                 start=(ic == 0), stop=(ic == 3))
                s_op.inc(t.last)
                s_oe.wait(t, 3 * oc + 3)
                for qt in range(16):
                    first = (oc == 0 and qt == 0)
                    lastm = (oc == NFC - 1 and qt == 15)
                    t.matmul(P1[:, qt:qt + 1], Fb[:, 128 * qt:128 * (qt + 1)],
                             self.ones_b[:], start=first, stop=False,
                             skip_group_check=True)
                    t.matmul(P1[:, 16 + qt:16 + qt + 1],
                             Fb[:, 2048 + 128 * qt:2048 + 128 * (qt + 1)],
                             self.ones_b[:], start=False, stop=lastm,
                             skip_group_check=True)
                s_op.inc(t.last)
        self.on("tensor", f_wo)

        def f_woe(s):
            for oc in range(NFC):
                s_op.wait(s, 2 * oc + 1)
                i = s.activation(Fb[:, 0:2048], P0[:], AF.Copy)
                s_oe.inc(i)
                i = s.activation(Fb[:, 2048:4096], Fb[:, 0:2048], AF.Square)
                s_oe.inc(i)
        self.on("scalar", f_woe)

        def f_wov(v):
            for oc in range(NFC):
                s_op.wait(v, 2 * oc + 1)
                s_oe.wait(v, 3 * oc + 1)
                i = v.tensor_copy(A[:, 2048 * oc:2048 * (oc + 1)], P0[:])
                s_oe.inc(i)
        self.on("vector", f_wov)
        self.done(s_oe); self.done(s_op)
        self.bar()
        if self.attn_trunc(6):
            return
        self.dump_sbuf("dbg_ao", A[:], [128, 8 * TOK], F32)

        s_or = self.sem("or")
        s_orD = self.sem("orD", "dmah")

        def f_oev(v):
            i = v.tensor_copy(Ff[:, 0:32], P1[:, 0:32])
            s_or.inc(i, 1)
        self.on("vector", f_oev)

        for stat in range(2):
            def f_tt(t, stat=stat):
                s_or.wait(t, 1 + 2 * stat)
                i = t.transpose(P2[0:16, 0:128], Ff[:, 16 * stat:16 * (stat + 1)],
                                self.eye_f[:])
                s_or.inc(i, 1)
            self.on("tensor", f_tt)

            def f_tv(v, stat=stat):
                s_or.wait(v, 2 + 2 * stat)
                if stat >= 1:
                    s_orD.wait(v, 32 * stat - 16)
                i = v.tensor_copy(self.tr16[:], P2[0:16, 0:128])
                s_or.inc(i, 1)
            self.on("vector", f_tv)

            def f_ts(sync, stat=stat):
                s_or.wait(sync, 3 + 2 * stat)
                i = sync.dma_start(out=self.rowb[:], in_=self.tr16[:])
                s_orD.inc(i, 16)
                s_orD.wait(sync, 32 * stat + 16)
                i = sync.dma_start(out=self.Drows[:, 2048 * stat:2048 * (stat + 1)],
                                   in_=self.rowb.rearrange("a b -> (a b)")[None, :])
                s_orD.inc(i, 16)
            self.on("sync", f_ts)

        mu_a = self.Drows[:, 0:2048]
        sq_a = self.Drows[:, 2048:4096]
        sr_a = self.Drows[:, 4096:6144]
        s_orr = self.sem("orr")

        def f_orow(v):
            s_or.wait(v, 5)
            s_orD.wait(v, 64)
            v.tensor_scalar(mu_a, mu_a, 1.0 / 1024.0, None, ALU.mult)
            v.tensor_scalar(sq_a, sq_a, 1.0 / 1024.0, None, ALU.mult)
            v.tensor_mul(sr_a, mu_a, mu_a)
            i = v.tensor_sub(sr_a, sq_a, sr_a)
            s_orr.inc(i)
        self.on("vector", f_orow)

        def f_oract(s):
            s_orr.wait(s, 1)
            i = s.activation(sr_a, sr_a, AF.Sqrt, bias=self.eps_c[0:1, :])
            s_orr.inc(i)
        self.on("scalar", f_oract)

        def f_orr(v):
            s_orr.wait(v, 2)
            i = v.reciprocal(sr_a, sr_a)
            s_orr.inc(i)
        self.on("vector", f_orr)

        def f_obc(t):
            s_orr.wait(t, 3)
            for sl in range(4):
                t.matmul(P0[:, 512 * sl:512 * (sl + 1)], self.onesr_f[:],
                         mu_a[:, 512 * sl:512 * (sl + 1)], start=True, stop=True)
            for sl in range(2):
                t.matmul(P1[:, 512 * sl:512 * (sl + 1)], self.onesr_f[:],
                         sr_a[:, 512 * sl:512 * (sl + 1)], start=True, stop=True)
                t.matmul(P2[:, 512 * sl:512 * (sl + 1)], self.onesr_f[:],
                         sr_a[:, 1024 + 512 * sl:1024 + 512 * (sl + 1)], start=True, stop=True)
            s_orr.inc(t.last)
        self.on("tensor", f_obc)
        orr_n = s_orr.n - s_orr.base

        self.scope("aapply")
        s_fx = self.sem("fx")
        s_fxl = self.sem("fxl", "dmah")
        ast = Df[:, 0:2048]
        xres = Ff[:, 0:2048]

        def f_fxs(sync):
            for fc in range(NFC):
                if fc > 0:
                    s_fx.wait(sync, 2 * fc)
                i = sync.dma_start(out=xres, in_=self.spx[:, 2048 * fc:2048 * (fc + 1)])
                s_fxl.inc(i, 16)
        self.on("sync", f_fxs)

        def f_fx(v):
            s_orr.wait(v, orr_n)
            for fc in range(NFC):
                ac = A[:, 2048 * fc:2048 * (fc + 1)]
                s_fxl.wait(v, 16 * fc + 16)
                v.tensor_sub(ast, ac, P0[:])
                for qh in range(2):
                    ps = P1 if qh == 0 else P2
                    v.tensor_mul(ast[:, 1024 * qh:1024 * (qh + 1)],
                                 ast[:, 1024 * qh:1024 * (qh + 1)], ps[:])
                i = v.tensor_scalar(ast, ast, self.aonw_c[:, fc:fc + 1],
                                    self.aonb_c[:, fc:fc + 1], ALU.mult, ALU.add)
                s_fx.inc(i, 1)
                i = v.tensor_add(ac, ast, xres)
                s_fx.inc(i, 1)
        self.on("vector", f_fx)
        self.done(s_fx); self.done(s_fxl)

        if last:
            s_fo = self.sem("fo")

            def f_fout(sync):
                s_fx.wait(sync, 2 * NFC)
                i = sync.dma_start(out=self.yT.rearrange("(c p) t -> p c t", p=128),
                                   in_=A.rearrange("p (c t) -> p c t", t=TOK))
                self.s_st.inc(i, 16)
            self.on("sync", f_fout)
            self.done(self.s_st)
        self.bar()
        self.release_scope()

    def build(self):
        with self.nc.Block() as block:
            self.block = block
            self.emit_setup()
            seq = [("ffn", 0), ("attn", 1), ("ffn", 2)][:self.depth_stop]
            for kind, li in seq:
                last = (li == len(seq) - 1 and li == self.depth_stop - 1)
                is_last = (li == self.depth_stop - 1)
                if kind == "ffn":
                    with self.nc.named_scope(f"ffn{li}"):
                        self.emit_ffn(li, is_last)
                else:
                    with self.nc.named_scope(f"attn{li}"):
                        self.emit_attn(is_last)
                    if not is_last:
                        pass
            if self.depth_stop == 2:
                pass
            st_n = self.s_st.n

            def f_fin(sync):
                self.s_st.wait_abs(sync, st_n)
            self.on("sync", f_fin)
        return self.nc


_CACHE = {}


def _get_nc(depth_stop=3):
    if depth_stop not in _CACHE:
        _CACHE[depth_stop] = MK(depth_stop).build()
    return _CACHE[depth_stop]


def kernel(**inputs):
    import ml_dtypes
    nc = _get_nc(3)
    x = np.asarray(inputs["L0_attn_x"])
    base = {
        "fWq": inputs["L0_ffn_Wq"], "fWk": inputs["L0_ffn_Wk"],
        "fWv": inputs["L0_ffn_Wv"], "fWo": inputs["L0_ffn_Wo"],
        "fbq": inputs["L0_ffn_bq"], "fbk": inputs["L0_ffn_bk"],
        "fbv": inputs["L0_ffn_bv"], "fbo": inputs["L0_ffn_bo"],
        "fWg": inputs["L0_ffn_Wg"], "fbg": inputs["L0_ffn_bg"],
        "fW1": inputs["L0_ffn_W1"], "fb1": inputs["L0_ffn_b1"],
        "fW2": inputs["L0_ffn_W2"], "fb2": inputs["L0_ffn_b2"],
        "aWq": inputs["L0_attn_Wq"], "aWkv": inputs["L0_attn_Wkv"],
        "aWo": inputs["L0_attn_Wo"],
        "anw": inputs["L0_attn_norm_w"], "anb": inputs["L0_attn_norm_b"],
        "aonw": inputs["L0_attn_out_norm_w"], "aonb": inputs["L0_attn_out_norm_b"],
        "eye": np.eye(128, dtype=np.float32),
    }
    base = {k: np.ascontiguousarray(np.asarray(v), dtype=np.float32)
            for k, v in base.items()}
    in_maps = []
    for r in range(NCORES):
        xs = np.ascontiguousarray(x[0, r * TOK:(r + 1) * TOK, :].T)
        in_maps.append({"xT": xs, **base})
    res = run_bass_kernel_spmd(nc, in_maps, core_ids=list(range(NCORES)))
    out = np.concatenate([res.results[r]["yT"].T for r in range(NCORES)], axis=0)
    return out.reshape(1, 16384, 1024).astype(np.float32)


# version 04
# speedup vs baseline: 1.0645x; 1.0645x over previous
import numpy as np
import concourse.bass as bass
import concourse.mybir as mybir
from concourse.bass_utils import run_bass_kernel_spmd

F32 = mybir.dt.float32
BF16 = mybir.dt.bfloat16
AF = mybir.ActivationFunctionType
ALU = mybir.AluOpType
AX = mybir.AxisListType

NCORES = 8
TOK = 2048
NFC = 8
EPS = 1e-5
DEX = 4096
NE = 8
SCL_F = 1.0 / 32.0
SCL_A = 0.125
KV_ROWS_F = 2048
KV_ROWS_A = 128
SB_BASE = 20480


class Sem:
    __slots__ = ("h", "base", "n")

    def __init__(self, h, base):
        self.h = h
        self.base = base
        self.n = base

    def inc(self, instr, k=1):
        instr.then_inc(self.h, k)
        self.n += k
        return instr

    def wait(self, eng, val=None):
        eng.wait_ge(self.h, self.n if val is None else self.base + val)

    def wait_abs(self, eng, val):
        eng.wait_ge(self.h, val)


ENGS = ("tensor", "vector", "scalar", "sync", "gpsimd")

_SKIP_LAST = {"nop", "engine_nop", "wait_ge", "wait_op", "sem_inc", "drain"}


class EngWrap:

    def __init__(self, eng):
        self._e = eng
        self.last = None

    def __getattr__(self, k):
        a = getattr(self._e, k)
        if callable(a) and not k.startswith("_"):
            def w(*args, __a=a, __k=k, **kw):
                r = __a(*args, **kw)
                if __k not in _SKIP_LAST and hasattr(r, "then_inc"):
                    self.last = r
                return r
            return w
        return a


class MK:
    def __init__(self, depth_stop=3, dbg=False, attn_stop=99):
        self.dbg = dbg
        self.attn_stop = attn_stop
        self.dbg_outs = {}
        nc = bass.Bass("TRN2", target_bir_lowering=False,
                       detect_race_conditions=False)
        self.nc = nc
        self.depth_stop = depth_stop
        self.pending = []

        di = lambda n, s: nc.dram_tensor(n, s, F32, kind="ExternalInput")
        self.xT = di("xT", [1024, TOK])
        self.fWq = di("fWq", [1024, 1024]); self.fWk = di("fWk", [1024, 1024])
        self.fWv = di("fWv", [1024, 1024]); self.fWo = di("fWo", [1024, 1024])
        self.fbq = di("fbq", [1024]); self.fbk = di("fbk", [1024])
        self.fbv = di("fbv", [1024]); self.fbo = di("fbo", [1024])
        self.fWg = di("fWg", [1024, 8]); self.fbg = di("fbg", [8])
        self.fW1 = di("fW1", [NE, 1024, DEX]); self.fb1 = di("fb1", [NE, DEX])
        self.fW2 = di("fW2", [NE, DEX, 1024]); self.fb2 = di("fb2", [NE, 1024])
        self.aWq = di("aWq", [1024, 512]); self.aWkv = di("aWkv", [1024, 128])
        self.aWo = di("aWo", [512, 1024])
        self.anw = di("anw", [1024]); self.anb = di("anb", [1024])
        self.aonw = di("aonw", [1024]); self.aonb = di("aonb", [1024])
        self.eye = di("eye", [128, 128])
        self.yT = nc.dram_tensor("yT", [1024, TOK], F32, kind="ExternalOutput")

        self.spx = nc.dram_tensor("spx", [128, 8 * TOK], F32)
        self.spu = nc.dram_tensor("spu", [128, 8 * TOK], F32)
        self.kvsl_f = nc.dram_tensor("kvsl_f", [KV_ROWS_F, TOK], BF16)
        self.kvslK = nc.dram_tensor("kvslK", [4096, 512], BF16)
        self.kvgK = [nc.dram_tensor(f"kvgK{q}", [NCORES * 1024, 512], BF16,
                                    addr_space="Shared") for q in range(4)]
        self.kvgV = [nc.dram_tensor(f"kvgV{q}", [NCORES * 256, TOK], BF16,
                                    addr_space="Shared") for q in range(4)]
        self.kvsl_a = nc.dram_tensor("kvsl_a", [KV_ROWS_A, TOK], BF16)
        self.kvg_a = nc.dram_tensor("kvg_a", [NCORES * KV_ROWS_A, TOK], BF16,
                                    addr_space="Shared")
        self.rowb = nc.dram_tensor("rowb", [16, 128], F32)

        def at(name, shape, dt, off):
            return nc.alloc_sbuf_tensor_at(name, shape, dt, offset=SB_BASE + off)
        self.A = at("A", [128, 8 * TOK], F32, 0)
        self.Ab = at("Ab", [128, 16 * TOK], BF16, 0)
        self.Bb = at("Bb", [128, 8 * TOK], BF16, 65536)
        self.B65 = at("B65", [65, 8 * TOK], BF16, 65536)
        self.Bf = at("Bf", [128, 4 * TOK], F32, 65536)
        self.Cb = at("Cb", [128, 8 * TOK], BF16, 98304)
        self.C65 = at("C65", [65, 8 * TOK], BF16, 98304)
        self.Db = at("Db", [128, 16896], BF16, 131072)
        self.Df = at("Df", [128, 8448], F32, 131072)
        self.Drows = at("Drows", [1, 8192], F32, 131072)
        self.Eb = at("Eb", [128, 8192], BF16, 164864)
        self.bvrow = at("bvrow", [1, 1024], BF16, 164864)
        self.Ff = at("Ff", [128, 4096], F32, 181248)
        self.Fb = at("Fb", [128, 8192], BF16, 181248)
        o = 197632
        def park(name, shape, dt):
            nonlocal o
            t = at(name, shape, dt, o)
            esz = 2 if dt == BF16 else 4
            o += ((shape[-1] * esz + 31) // 32) * 32
            return t
        self.eye_f = park("eye_f", [128, 128], F32)
        self.bq_c = park("bq_c", [128, 8], F32)
        self.bk_c = park("bk_c", [128, 8], F32)
        self.bo_c = park("bo_c", [128, 8], F32)
        self.anw_c = park("anw_c", [128, 8], F32)
        self.anb_c = park("anb_c", [128, 8], F32)
        self.aonw_c = park("aonw_c", [128, 8], F32)
        self.aonb_c = park("aonb_c", [128, 8], F32)
        self.b1T = park("b1T", [128, 8 * 32], F32)
        self.b2sb = park("b2sb", [8, 1024], BF16)
        self.wg_sb = park("wg_sb", [128, 64], BF16)
        self.bgrow = park("bgrow", [1, 8], BF16)
        self.gTM = park("gTM", [128, 16 * 8], F32)
        self.ones_b = park("ones_b", [128, 1], BF16)
        self.ones_f = park("ones_f", [128, 1], F32)
        self.onesr_f = park("onesr_f", [1, 128], F32)
        self.onesr_b = park("onesr_b", [1, 512], BF16)
        self.Mrow = park("Mrow", [1, 16], F32)
        self.tr16 = park("tr16", [16, 128], F32)
        self.negC = park("negC", [1, TOK], BF16)
        self.ones2d = park("ones2d", [128, 64], F32)
        self.eps_c = park("eps_c", [128, 1], F32)
        assert SB_BASE + o <= 229344, f"sbuf overflow: {SB_BASE + o}"

        self.P0 = nc.alloc_psum_tensor("P0", [128, 2048], F32)
        self.P1 = nc.alloc_psum_tensor("P1", [128, 1024], F32)
        self.P2 = nc.alloc_psum_tensor("P2", [128, 1024], F32)

        self._semct = 0
        self._free = {"eng": [], "dmah": [], "dmas": [], "cc": []}
        self._hw_n = {}
        self._scope = []
        self.s_st = self.sem("st", "dmah")
        self._scope = []
        self.block = None

    def attn_trunc(self, stage):
        if self.attn_stop > stage:
            return False

        def f(sync):
            i = sync.dma_start(out=self.yT[:],
                               in_=self.spx[:, 0:TOK].rearrange("p t -> p t"))
            self.s_st.inc(i, 16)
        def f2(sync):
            i = sync.dma_start(out=self.yT.rearrange("(c p) t -> p c t", p=128),
                               in_=self.spx.rearrange("p (c t) -> p c t", t=TOK))
            self.s_st.inc(i, 16)
        self.on("sync", f2)
        self.bar()
        return True

    def dump_dram(self, name, src_ap, shape, dt):
        if not self.dbg:
            return
        t = self.nc.dram_tensor(name, shape, dt, kind="ExternalOutput")
        self.dbg_outs[name] = t

        def f(sync):
            i = sync.dma_start(out=t[:], in_=src_ap)
            self.s_st.inc(i, 16)
        self.on("sync", f)

    def dump_sbuf(self, name, src_ap, shape, dt):
        if not self.dbg:
            return
        t = self.nc.dram_tensor(name, shape, dt, kind="ExternalOutput")
        self.dbg_outs[name] = t

        def f(sync):
            i = sync.dma_start(out=t[:], in_=src_ap)
            self.s_st.inc(i, 16)
        self.on("sync", f)
        self.bar()

    def sem(self, tag, kind="eng"):
        self._semct += 1
        free = self._free[kind]
        if free:
            h = free.pop()
        else:
            h = self.nc.alloc_semaphore(f"s{self._semct}_{tag}")
            self._hw_n[h] = 0
        s = Sem(h, self._hw_n[h])
        self._scope.append((kind, s))
        return s

    def release_scope(self):
        for kind, s in self._scope:
            self._hw_n[s.h] = s.n
            self._free[kind].append(s.h)
        self._scope = []

    def on(self, engname, f):
        if getattr(self, "_scope_name", None):
            with self.nc.named_scope(self._scope_name):
                getattr(self.block, engname)(lambda eng: f(EngWrap(eng)))
        else:
            getattr(self.block, engname)(lambda eng: f(EngWrap(eng)))

    def scope(self, name):
        self._scope_name = name

    def done(self, sem):
        self.pending.append((sem, sem.n))

    def bar(self):
        pend = [(s, v) for (s, v) in self.pending if v > 0]
        self.pending = []
        for c in ENGS:
            def f(eng, pend=pend):
                for s, v in pend:
                    s.wait_abs(eng, v)
            self.on(c, f)

    def emit_setup(self):
        s_ld = self.sem("ld", "dmah")
        s_tr = self.sem("tr", "dmah")
        s_g = self.sem("g", "dmas")
        s_gm = self.sem("gm")
        s_pe = self.sem("pe")
        s_dve = self.sem("dve")

        def f_sync(sync):
            s_ld.inc(sync.dma_start(out=self.A.rearrange("p (c t) -> p c t", t=TOK),
                                    in_=self.xT.rearrange("(c p) t -> p c t", p=128)), 16)
            s_ld.inc(sync.dma_start(out=self.eye_f[:], in_=self.eye[:]), 16)
        self.on("sync", f_sync)

        def f_g(g):
            s_g.inc(g.dma_start(out=self.b2sb[:], in_=self.fb2[:]), 16)
            s_g.inc(g.dma_start(out=self.wg_sb.rearrange("p (c e) -> p c e", e=8),
                                in_=self.fWg.rearrange("(c p) e -> p c e", p=128)), 16)
            s_g.inc(g.dma_start(out=self.bgrow[:], in_=self.fbg[None, :]), 16)
            s_g.inc(g.dma_start(out=self.bvrow[:], in_=self.fbv[None, :]), 16)
            g.memset(self.ones_b[:], 1.0)
            g.memset(self.ones_f[:], 1.0)
            g.memset(self.onesr_f[:], 1.0)
            g.memset(self.ones2d[:], 1.0)
            g.memset(self.eps_c[:], EPS)
            s_gm.inc(g.memset(self.onesr_b[:], 1.0), 1)
        self.on("gpsimd", f_g)

        vecs = [(self.fbq, self.bq_c), (self.fbk, self.bk_c), (self.fbo, self.bo_c),
                (self.anw, self.anw_c), (self.anb, self.anb_c),
                (self.aonw, self.aonw_c), (self.aonb, self.aonb_c)]
        for src, dst in vecs:
            def f_s(sync, src=src):
                s_pe.wait(sync)
                s_tr.inc(sync.dma_start(out=self.tr16[0:8, :],
                                        in_=src.rearrange("(c p) -> c p", p=128)), 16)
            self.on("sync", f_s)

            def f_t(t):
                s_ld.wait(t)
                s_tr.wait(t)
                s_dve.wait(t)
                s_pe.inc(t.transpose(self.P1[:, 0:8], self.tr16[0:8, :],
                                     self.eye_f[0:8, 0:8]))
            self.on("tensor", f_t)

            def f_v(v, dst=dst):
                s_pe.wait(v)
                s_dve.inc(v.tensor_copy(dst[:], self.P1[:, 0:8]))
            self.on("vector", f_v)

        for e in range(NE):
            for half in range(2):
                def f_s(sync, e=e, half=half):
                    s_pe.wait(sync)
                    i = sync.dma_start(
                        out=self.tr16[:],
                        in_=self.fb1[e].rearrange("(a b) -> a b", b=128)[16 * half:16 * (half + 1), :])
                    s_tr.inc(i, 16)
                self.on("sync", f_s)

                def f_t(t):
                    s_tr.wait(t)
                    s_dve.wait(t)
                    s_pe.inc(t.transpose(self.P1[:, 0:16], self.tr16[:],
                                         self.eye_f[0:16, 0:16]))
                self.on("tensor", f_t)

                def f_v(v, e=e, half=half):
                    s_pe.wait(v)
                    c0 = 32 * e + 16 * half
                    s_dve.inc(v.tensor_copy(self.b1T[:, c0:c0 + 16], self.P1[:, 0:16]))
                self.on("vector", f_v)

        for s in (s_ld, s_tr, s_g, s_gm, s_pe, s_dve):
            self.done(s)
        self.bar()
        self.release_scope()

    def emit_ln(self, mode):
        P0, P1, P2 = self.P0, self.P1, self.P2
        mu = self.Drows[:, 0:2048]
        sr = self.Drows[:, 2048:4096]
        tp = self.Drows[:, 4096:6144]
        s_c = self.sem("lncast")
        s_p = self.sem("lnpe")
        s_q = self.sem("lnsq")
        s_r = self.sem("lnrow")

        def f_cast(v):
            for fc in range(NFC):
                i = v.tensor_copy(self.Bb[:, 2048 * fc:2048 * (fc + 1)],
                                  self.A[:, 2048 * fc:2048 * (fc + 1)])
            s_c.inc(i)
        self.on("vector", f_cast)

        def f_sumx(t):
            s_c.wait(t)
            for fc in range(NFC):
                for qh in range(2):
                    ps = P1 if qh == 0 else P2
                    for sl in range(2):
                        c0 = 2048 * fc + 1024 * qh + 512 * sl
                        t.matmul(ps[0:1, 512 * sl:512 * (sl + 1)], self.ones_b[:],
                                 self.Bb[:, c0:c0 + 512],
                                 start=(fc == 0), stop=(fc == NFC - 1),
                                 skip_group_check=True)
            s_p.inc(t.last)
        self.on("tensor", f_sumx)

        def f_muev(v):
            s_p.wait(v, 1)
            v.tensor_scalar(mu[:, 0:1024], P1[0:1, :], 1.0 / 1024.0, None, ALU.mult)
            i = v.tensor_scalar(mu[:, 1024:2048], P2[0:1, :], 1.0 / 1024.0, None, ALU.mult)
            s_r.inc(i)
        self.on("vector", f_muev)

        def f_sq(s):
            s_c.wait(s)
            for fc in range(NFC):
                if fc >= 2:
                    s_p.wait(s, fc)
                i = s.activation(self.Fb[:, 2048 * (fc % 2):2048 * (fc % 2 + 1)],
                                 self.Bb[:, 2048 * fc:2048 * (fc + 1)], AF.Square)
                s_q.inc(i)
        self.on("scalar", f_sq)

        def f_sumsq(t):
            s_r.wait(t, 1)
            for fc in range(NFC):
                s_q.wait(t, fc + 1)
                for qh in range(2):
                    ps = P1 if qh == 0 else P2
                    for sl in range(2):
                        c0 = 2048 * (fc % 2) + 1024 * qh + 512 * sl
                        t.matmul(ps[0:1, 512 * sl:512 * (sl + 1)], self.ones_b[:],
                                 self.Fb[:, c0:c0 + 512],
                                 start=(fc == 0), stop=(fc == NFC - 1),
                                 skip_group_check=True)
                s_p.inc(t.last)
        self.on("tensor", f_sumsq)

        def f_var(v):
            s_p.wait(v, 1 + NFC)
            v.tensor_scalar(sr[:, 0:1024], P1[0:1, :], 1.0 / 1024.0, None, ALU.mult)
            v.tensor_scalar(sr[:, 1024:2048], P2[0:1, :], 1.0 / 1024.0, None, ALU.mult)
            v.tensor_mul(tp, mu, mu)
            i = v.tensor_sub(sr, sr, tp)
            s_r.inc(i)
        self.on("vector", f_var)

        def f_rs(s):
            s_r.wait(s, 2)
            i = s.activation(tp, sr, AF.Sqrt, bias=self.eps_c[0:1, :])
            s_q.inc(i)
        self.on("scalar", f_rs)

        def f_r(v):
            s_q.wait(v, NFC + 1)
            i = v.reciprocal(tp, tp)
            s_r.inc(i)
        self.on("vector", f_r)

        if mode == "double":
            def f_v2(v):
                s_r.wait(v, 3)
                v.tensor_mul(sr, sr, tp)
                i = v.tensor_mul(sr, sr, tp)
                s_r.inc(i)
            self.on("vector", f_v2)

            def f_rs2(s):
                s_r.wait(s, 4)
                i = s.activation(sr, sr, AF.Sqrt, bias=self.eps_c[0:1, :])
                s_q.inc(i)
            self.on("scalar", f_rs2)

            def f_r2(v):
                s_q.wait(v, NFC + 2)
                i = v.reciprocal(sr, sr)
                s_r.inc(i)
                s_r.wait(v, 5)
                i = v.tensor_mul(sr, sr, tp)
                s_r.inc(i)
            self.on("vector", f_r2)
        else:
            def f_cp(v):
                s_r.wait(v, 3)
                i = v.tensor_copy(sr, tp)
                s_r.inc(i)
            self.on("vector", f_cp)
        rows_n = s_r.n - s_r.base

        def f_bc(t):
            s_r.wait(t, rows_n)
            for sl in range(4):
                t.matmul(P0[:, 512 * sl:512 * (sl + 1)], self.onesr_f[:],
                         mu[:, 512 * sl:512 * (sl + 1)], start=True, stop=True)
            for sl in range(2):
                t.matmul(P1[:, 512 * sl:512 * (sl + 1)], self.onesr_f[:],
                         sr[:, 512 * sl:512 * (sl + 1)], start=True, stop=True)
                t.matmul(P2[:, 512 * sl:512 * (sl + 1)], self.onesr_f[:],
                         sr[:, 1024 + 512 * sl:1024 + 512 * (sl + 1)], start=True, stop=True)
            s_p.inc(t.last)
        self.on("tensor", f_bc)
        self.done(s_p)
        return s_p

    def emit_ffn(self, li, last):
        P0, P1, P2 = self.P0, self.P1, self.P2
        A, Ab, Bb, Bf, Cb = self.A, self.Ab, self.Bb, self.Bf, self.Cb
        Db, Df, Eb, Ff, Fb = self.Db, self.Df, self.Eb, self.Ff, self.Fb

        self.scope("ln")
        s_lnp = self.emit_ln("plain")
        lnp_n = s_lnp.n - s_lnp.base
        stage = Ff[:, 2048:4096]
        s_app = self.sem("app")
        s_spl = self.sem("spl", "dmah")

        def f_app(v):
            s_lnp.wait(v, lnp_n)
            for fc in range(NFC):
                xc = A[:, 2048 * fc:2048 * (fc + 1)]
                if fc > 0:
                    s_spl.wait(v, 32 * fc)
                i = v.tensor_sub(stage, xc, P0[:])
                s_app.inc(i)
                for qh in range(2):
                    ps = P1 if qh == 0 else P2
                    v.tensor_mul(stage[:, 1024 * qh:1024 * (qh + 1)],
                                 stage[:, 1024 * qh:1024 * (qh + 1)], ps[:])
                i = v.tensor_copy(Bb[:, 2048 * fc:2048 * (fc + 1)], stage)
                s_app.inc(i)
        self.on("vector", f_app)

        def f_spl(sync):
            for fc in range(NFC):
                s_app.wait(sync, 2 * fc + 1)
                s_spl.inc(sync.dma_start(out=self.spx[:, 2048 * fc:2048 * (fc + 1)],
                                         in_=A[:, 2048 * fc:2048 * (fc + 1)]), 16)
                s_app.wait(sync, 2 * fc + 2)
                s_spl.inc(sync.dma_start(out=self.spu[:, 2048 * fc:2048 * (fc + 1)],
                                         in_=stage), 16)
        self.on("sync", f_spl)
        self.done(s_app); self.done(s_spl)
        self.bar()
        self.dump_dram(f"dbg_u{li}", self.spu[:], [128, 8 * TOK], F32)
        self.dump_dram(f"dbg_x{li}", self.spx[:], [128, 8 * TOK], F32)

        self.scope("proj")
        WA = Db[:, 0:8192]
        WB = Db[:, 8192:16384]
        s_wl = self.sem("wl", "dmas")

        def f_wdma(g):
            i = g.dma_start(out=WA.rearrange("p (c n) -> p c n", n=1024),
                            in_=self.fWq.rearrange("(c p) n -> p c n", p=128))
            s_wl.inc(i, 16)
            i = g.dma_start(out=WB.rearrange("p (c n) -> p c n", n=1024),
                            in_=self.fWv.rearrange("(c p) n -> p c n", p=128))
            s_wl.inc(i, 16)
            i = g.dma_start(out=self.bvrow[:], in_=self.fbv[None, :])
            s_wl.inc(i, 16)
        self.on("gpsimd", f_wdma)

        s_qp = self.sem("qp")
        s_qe = self.sem("qe")

        def f_q(t):
            s_wl.wait(t, 16)
            for oc in range(NFC):
                if oc > 0:
                    s_qe.wait(t, oc)
                for sl in range(4):
                    for ic in range(NFC):
                        t.matmul(P0[:, 512 * sl:512 * (sl + 1)],
                                 WA[:, 1024 * ic + 128 * oc:1024 * ic + 128 * oc + 128],
                                 Bb[:, 2048 * ic + 512 * sl:2048 * ic + 512 * (sl + 1)],
                                 start=(ic == 0), stop=(ic == NFC - 1))
                s_qp.inc(t.last)
        self.on("tensor", f_q)

        def f_qe(s):
            for oc in range(NFC):
                s_qp.wait(s, oc + 1)
                i = s.activation(Cb[:, 2048 * oc:2048 * (oc + 1)], P0[:], AF.Identity,
                                 bias=self.bq_c[:, oc:oc + 1])
                s_qe.inc(i)
        self.on("scalar", f_qe)

        s_vp = self.sem("vp")
        s_ve = self.sem("ve")
        s_vs = self.sem("vs", "dmah")
        vstage = Fb[:, 0:1024]

        def f_v(t):
            s_wl.wait(t, 48)
            s_qe.wait(t, NFC)
            for tt in range(16):
                ps = P1 if tt % 2 == 0 else P2
                if tt >= 2:
                    s_ve.wait(t, tt - 1)
                for sl in range(2):
                    for ic in range(NFC):
                        t.matmul(ps[:, 512 * sl:512 * (sl + 1)],
                                 Bb[:, 2048 * ic + 128 * tt:2048 * ic + 128 * tt + 128],
                                 WB[:, 1024 * ic + 512 * sl:1024 * ic + 512 * (sl + 1)],
                                 start=(ic == 0), stop=False)
                    t.matmul(ps[:, 512 * sl:512 * (sl + 1)], self.onesr_b[:, 0:128],
                             self.bvrow[:, 512 * sl:512 * (sl + 1)],
                             start=False, stop=True)
                s_vp.inc(t.last)
        self.on("tensor", f_v)

        def f_ve(s):
            for tt in range(16):
                ps = P1 if tt % 2 == 0 else P2
                s_vp.wait(s, tt + 1)
                if tt > 0:
                    s_vs.wait(s, 16 * tt)
                i = s.activation(vstage, ps[:], AF.Copy)
                s_ve.inc(i)
        self.on("scalar", f_ve)

        def f_vs(sync):
            for tt in range(16):
                s_ve.wait(sync, tt + 1)
                dst = self.kvsl_f[1024 + 64 * tt:1024 + 64 * (tt + 1), :]
                i = sync.dma_start(
                    out=dst.rearrange("a n -> (a n)").rearrange("(p f) -> p f", f=1024),
                    in_=vstage)
                s_vs.inc(i, 16)
        self.on("sync", f_vs)

        s_wk = self.sem("wk", "dmas")

        def f_wk(g):
            s_qp.wait(g, NFC)
            i = g.dma_start(out=WA.rearrange("p (c n) -> p c n", n=1024),
                            in_=self.fWk.rearrange("(c p) n -> p c n", p=128))
            s_wk.inc(i, 16)
        self.on("gpsimd", f_wk)

        kstage = Fb[:, 1024:3072]
        ksq = Fb[:, 6144:8192]
        s_kp = self.sem("kp")
        s_ke = self.sem("ke")
        s_ks = self.sem("ks", "dmah")
        s_kn = self.sem("kn")

        def f_k(t):
            s_wk.wait(t, 16)
            s_ve.wait(t, 16)
            for oc in range(NFC):
                if oc > 0:
                    s_ke.wait(t, 2 * oc)
                for sl in range(4):
                    for ic in range(NFC):
                        t.matmul(P0[:, 512 * sl:512 * (sl + 1)],
                                 WA[:, 1024 * ic + 128 * oc:1024 * ic + 128 * oc + 128],
                                 Bb[:, 2048 * ic + 512 * sl:2048 * ic + 512 * (sl + 1)],
                                 start=(ic == 0), stop=(ic == NFC - 1))
                s_kp.inc(t.last)
                s_ke.wait(t, 2 * oc + 2)
                for qh in range(2):
                    ps = P1 if qh == 0 else P2
                    for sl in range(2):
                        c0 = 1024 * qh + 512 * sl
                        t.matmul(ps[0:1, 512 * sl:512 * (sl + 1)], self.ones_b[:],
                                 ksq[:, c0:c0 + 512],
                                 start=(oc == 0), stop=(oc == NFC - 1),
                                 skip_group_check=True)
                s_kp.inc(t.last)
        self.on("tensor", f_k)

        def f_ke(s):
            for oc in range(NFC):
                s_kp.wait(s, 2 * oc + 1)
                if oc > 0:
                    s_ks.wait(s, 64 * oc)
                i = s.activation(kstage, P0[:], AF.Identity, bias=self.bk_c[:, oc:oc + 1])
                s_ke.inc(i)
                i = s.activation(ksq, kstage, AF.Square)
                s_ke.inc(i)
        self.on("scalar", f_ke)

        def f_ks(sync):
            for oc in range(NFC):
                s_ke.wait(sync, 2 * oc + 1)
                for q in range(4):
                    i = sync.dma_start(
                        out=self.kvslK[1024 * q + 128 * oc:1024 * q + 128 * (oc + 1), :],
                        in_=kstage[:, 512 * q:512 * (q + 1)])
                    s_ks.inc(i, 16)
        self.on("sync", f_ks)

        def f_kn(v):
            s_kp.wait(v, 2 * NFC)
            v.tensor_copy(self.negC[:, 0:1024], P1[0:1, :])
            i = v.tensor_copy(self.negC[:, 1024:2048], P2[0:1, :])
            s_kn.inc(i)
            s_kn.wait(v, 1)
            i = v.reduce_max(self.Mrow[:, 0:1], self.negC[:], axis=AX.X)
            s_kn.inc(i)
        self.on("vector", f_kn)

        self.scope("gather")
        s_cc = self.sem("cc", "cc")
        ks_n = s_ks.n - s_ks.base

        def f_cc(g):
            for q in range(4):
                s_vs.wait(g, 16 * (4 * q + 4))
                i = g.collective_compute("AllGather", ALU.bypass,
                                         replica_groups=[list(range(NCORES))],
                                         ins=[self.kvsl_f[1024 + 256 * q:1280 + 256 * q, :]],
                                         outs=[self.kvgV[q][:]])
                s_cc.inc(i)
            for q in range(4):
                if q == 0:
                    s_ks.wait(g, ks_n)
                i = g.collective_compute("AllGather", ALU.bypass,
                                         replica_groups=[list(range(NCORES))],
                                         ins=[self.kvslK[1024 * q:1024 * (q + 1), :]],
                                         outs=[self.kvgK[q][:]])
                s_cc.inc(i)
        self.on("gpsimd", f_cc)

        qn = self.Drows[:, 0:2048]
        s_qn = self.sem("qn")
        s_qs = self.sem("qs")

        def f_qsq(s):
            s_kp.wait(s, 2 * NFC)
            s_ks.wait(s, 16 * NFC)
            for fc in range(NFC):
                if fc > 0:
                    s_qn.wait(s, fc)
                i = s.activation(ksq, Cb[:, 2048 * fc:2048 * (fc + 1)], AF.Square)
                s_qs.inc(i)
        self.on("scalar", f_qsq)

        def f_qnt(t):
            s_kn.wait(t, 1)
            for fc in range(NFC):
                s_qs.wait(t, fc + 1)
                for qh in range(2):
                    ps = P1 if qh == 0 else P2
                    for sl in range(2):
                        c0 = 1024 * qh + 512 * sl
                        t.matmul(ps[0:1, 512 * sl:512 * (sl + 1)], self.ones_b[:],
                                 ksq[:, c0:c0 + 512],
                                 start=(fc == 0), stop=(fc == NFC - 1),
                                 skip_group_check=True)
                s_qn.inc(t.last)
        self.on("tensor", f_qnt)
        qn_pe = s_qn.n - s_qn.base

        def f_qnv(v):
            s_qn.wait(v, qn_pe)
            v.tensor_copy(qn[:, 0:1024], P1[0:1, :])
            i = v.tensor_copy(qn[:, 1024:2048], P2[0:1, :])
            s_qn.inc(i)
        self.on("vector", f_qnv)

        s_m = self.sem("m")

        def f_mv2(v):
            s_kn.wait(v, 2)
            s_qn.wait(v, qn_pe + 1)
            i = v.tensor_scalar(qn, qn, self.Mrow[:, 0:1], None, ALU.mult)
            s_m.inc(i)
        self.on("vector", f_mv2)

        def f_ma(s):
            s_m.wait(s, 1)
            i = s.activation(qn, qn, AF.Sqrt)
            s_m.inc(i)
        self.on("scalar", f_ma)

        def f_mv3(v):
            s_m.wait(v, 2)
            i = v.tensor_scalar(self.negC[:], qn, -1.0, None, ALU.mult)
            s_m.inc(i)
        self.on("vector", f_mv3)
        self.done(s_m); self.done(s_qe)
        self.bar()
        self.dump_sbuf(f"dbg_negC{li}", self.negC[:], [1, TOK], BF16)
        self.dump_sbuf(f"dbg_q{li}", Cb[:], [128, 8 * TOK], BF16)

        self.scope("fattn")
        KD = [Db[:, 0:4096], Db[:, 4096:8192]]
        VD = [Db[:, 8192:12288], Db[:, 12288:16384]]
        s_kv = self.sem("kv", "dmah")
        s_sc = self.sem("sc")
        s_ex = self.sem("ex")
        s_avp = self.sem("avp")
        s_avd = self.sem("avd")

        def f_kvs(sync):
            for qr in range(32):
                q, blk, par = qr // 8, qr % 8, qr % 2
                if qr >= 2:
                    s_avp.wait(sync, 32 * (qr - 1))
                if blk == 0:
                    s_cc.wait(sync, 5 + q)
                for fc in range(NFC):
                    i = sync.dma_start(
                        out=KD[par].rearrange("p (c n) -> p c n", n=512)[:, fc, :],
                        in_=self.kvgK[q][1024 * blk + 128 * fc:
                                         1024 * blk + 128 * (fc + 1), :])
                    s_kv.inc(i, 16)
                for ktl in range(4):
                    src = self.kvgV[q][256 * blk + 64 * ktl:256 * blk + 64 * (ktl + 1), :]
                    i = sync.dma_start(
                        out=VD[par].rearrange("p (c n) -> p c n", n=1024)[:, ktl, :],
                        in_=src.rearrange("a n -> (a n)").rearrange("(p f) -> p f", f=1024))
                    s_kv.inc(i, 16)
        self.on("sync", f_kvs)

        def sums_mm(t, qr, ktl, qh):
            for qt in range(8):
                e0 = 2048 * ktl + 1024 * qh + 128 * qt
                col = 8 * qh + qt
                first = (qr == 0 and ktl == 0 and qh == 0 and qt == 0)
                lastm = (qr == 31 and ktl == 3 and qh == 1 and qt == 7)
                t.matmul(P2[:, col:col + 1], Eb[:, e0:e0 + 128], self.ones_b[:],
                         start=first, stop=lastm, skip_group_check=True)

        def f_att(t):
            for qr in range(32):
                par = qr % 2
                s_kv.wait(t, 192 * qr + 128)
                for i8 in range(8):
                    ktl, qh = i8 // 2, i8 % 2
                    g = 8 * qr + i8
                    half = P0[:, 0:1024] if i8 % 2 == 0 else P0[:, 1024:2048]
                    if g >= 2:
                        s_ex.wait(t, g - 1)
                    for fc in range(NFC):
                        for sl in range(2):
                            c0 = 2048 * fc + 1024 * qh + 512 * sl
                            t.matmul(half[:, 512 * sl:512 * (sl + 1)],
                                     KD[par].rearrange("p (c n) -> p c n", n=512)[:, fc, 128 * ktl:128 * (ktl + 1)],
                                     Cb[:, c0:c0 + 512],
                                     start=(fc == 0), stop=False)
                    for sl in range(2):
                        t.matmul(half[:, 512 * sl:512 * (sl + 1)],
                                 self.onesr_b[:, 0:128],
                                 self.negC[:, 1024 * qh + 512 * sl:1024 * qh + 512 * (sl + 1)],
                                 start=False, stop=True)
                    s_sc.inc(t.last)
                    if i8 >= 1:
                        s_ex.wait(t, g)
                        sums_mm(t, qr, (i8 - 1) // 2, (i8 - 1) % 2)
                s_ex.wait(t, 8 * (qr + 1))
                sums_mm(t, qr, 3, 1)
                s_kv.wait(t, 192 * (qr + 1))
                for fc in range(NFC):
                    for qh in range(2):
                        for sl in range(2):
                            gidx = 32 * qr + 4 * fc + 2 * qh + sl
                            bank = 512 * (gidx % 2)
                            if gidx >= 2:
                                s_avd.wait(t, gidx - 1)
                            for ktl in range(4):
                                e0 = 2048 * ktl + 1024 * qh + 512 * sl
                                t.matmul(P1[:, bank:bank + 512],
                                         VD[par].rearrange("p (c n) -> p c n", n=1024)[:, ktl, 128 * fc:128 * (fc + 1)],
                                         Eb[:, e0:e0 + 512],
                                         start=(ktl == 0), stop=(ktl == 3))
                            s_avp.inc(t.last)
        self.on("tensor", f_att)

        def f_exp(s):
            for qr in range(32):
                for i8 in range(8):
                    ktl, qh = i8 // 2, i8 % 2
                    g = 8 * qr + i8
                    half = P0[:, 0:1024] if i8 % 2 == 0 else P0[:, 1024:2048]
                    s_sc.wait(s, g + 1)
                    if qr >= 1:
                        s_avp.wait(s, 32 * qr)
                    e0 = 2048 * ktl + 1024 * qh
                    i = s.activation(Eb[:, e0:e0 + 1024], half, AF.Exp, scale=SCL_F)
                    s_ex.inc(i)
        self.on("scalar", f_exp)

        def f_avd(v):
            for qr in range(32):
                for fc in range(NFC):
                    for qh in range(2):
                        for sl in range(2):
                            gidx = 32 * qr + 4 * fc + 2 * qh + sl
                            bank = 512 * (gidx % 2)
                            s_avp.wait(v, gidx + 1)
                            c0 = 2048 * fc + 1024 * qh + 512 * sl
                            dst = A[:, c0:c0 + 512]
                            if qr == 0:
                                i = v.tensor_copy(dst, P1[:, bank:bank + 512])
                            else:
                                i = v.tensor_add(dst, dst, P1[:, bank:bank + 512])
                            s_avd.inc(i)
        self.on("vector", f_avd)
        self.done(s_avp); self.done(s_avd)
        self.bar()
        self.dump_sbuf(f"dbg_av{li}", A[:], [128, 8 * TOK], F32)

        self.scope("norm")
        s_nr = self.sem("nr")
        s_nrD = self.sem("nrD", "dmah")

        def f_rc(v):
            i = v.reciprocal(Ff[:, 0:16], P2[:, 0:16])
            s_nr.inc(i)
        self.on("vector", f_rc)

        def f_rt(t):
            s_nr.wait(t, 1)
            i = t.transpose(P1[0:16, 0:128], Ff[:, 0:16], self.eye_f[:])
            s_nr.inc(i)
        self.on("tensor", f_rt)

        def f_rv(v):
            s_nr.wait(v, 2)
            i = v.tensor_copy(self.tr16[:], P1[0:16, 0:128])
            s_nr.inc(i)
        self.on("vector", f_rv)

        def f_rb(sync):
            s_nr.wait(sync, 3)
            i = sync.dma_start(out=self.rowb[:], in_=self.tr16[:])
            s_nrD.inc(i, 16)
            s_nrD.wait(sync, 16)
            i = sync.dma_start(out=self.Drows[:, 0:2048],
                               in_=self.rowb.rearrange("a b -> (a b)")[None, :])
            s_nrD.inc(i, 16)
        self.on("sync", f_rb)

        def f_rbc(t):
            s_nrD.wait(t, 32)
            for sl in range(4):
                t.matmul(P0[:, 512 * sl:512 * (sl + 1)], self.onesr_f[:],
                         self.Drows[:, 512 * sl:512 * (sl + 1)], start=True, stop=True)
            s_nr.inc(t.last)
        self.on("tensor", f_rbc)

        def f_nm(v):
            s_nr.wait(v, 4)
            for fc in range(NFC):
                i = v.tensor_mul(Bb[:, 2048 * fc:2048 * (fc + 1)],
                                 A[:, 2048 * fc:2048 * (fc + 1)], P0[:])
            s_nr.inc(i)
        self.on("vector", f_nm)
        nm_n = s_nr.n - s_nr.base

        self.scope("wo")
        s_wo = self.sem("wo", "dmas")

        def f_wodma(g):
            s_avp.wait(g, 1024)
            i = g.dma_start(out=Eb.rearrange("p (c n) -> p c n", n=1024),
                            in_=self.fWo.rearrange("(c p) n -> p c n", p=128))
            s_wo.inc(i, 16)
        self.on("gpsimd", f_wodma)

        s_ur = self.sem("ur", "dmah")

        def f_url(sync):
            s_nr.wait(sync, nm_n)
            for fc in range(NFC):
                i = sync.dma_start(out=A[:, 2048 * fc:2048 * (fc + 1)],
                                   in_=self.spu[:, 2048 * fc:2048 * (fc + 1)])
                s_ur.inc(i, 16)
        self.on("sync", f_url)

        s_u2p = self.sem("u2p")
        s_u2d = self.sem("u2d")

        def f_wo(t):
            s_wo.wait(t, 16)
            s_nr.wait(t, nm_n)
            for oc in range(NFC):
                if oc > 0:
                    s_u2d.wait(t, 2 * oc)
                for sl in range(4):
                    for ic in range(NFC):
                        t.matmul(P0[:, 512 * sl:512 * (sl + 1)],
                                 Eb[:, 1024 * ic + 128 * oc:1024 * ic + 128 * oc + 128],
                                 Bb[:, 2048 * ic + 512 * sl:2048 * ic + 512 * (sl + 1)],
                                 start=(ic == 0), stop=(ic == NFC - 1))
                s_u2p.inc(t.last)
        self.on("tensor", f_wo)

        def f_u2(v):
            s_ur.wait(v, 16 * NFC)
            for oc in range(NFC):
                s_u2p.wait(v, oc + 1)
                uc = A[:, 2048 * oc:2048 * (oc + 1)]
                i = v.scalar_tensor_tensor(uc, P0[:], self.bo_c[:, oc:oc + 1], uc,
                                           ALU.add, ALU.add)
                s_u2d.inc(i)
                i = v.tensor_copy(Db[:, 2048 * oc:2048 * (oc + 1)], uc)
                s_u2d.inc(i)
        self.on("vector", f_u2)
        self.done(s_u2d)
        self.bar()
        self.dump_sbuf(f"dbg_u2_{li}", A[:], [128, 8 * TOK], F32)

        self.scope("gates")
        s_gt = self.sem("gt")
        s_gr = self.sem("gr")

        def f_gsc(t):
            for sl in range(4):
                for ic in range(NFC):
                    t.matmul(P0[0:8, 512 * sl:512 * (sl + 1)],
                             self.wg_sb[:, 8 * ic:8 * (ic + 1)],
                             Db[:, 2048 * ic + 512 * sl:2048 * ic + 512 * (sl + 1)],
                             start=(ic == 0), stop=False)
                t.matmul(P0[0:8, 512 * sl:512 * (sl + 1)], self.bgrow[:],
                         self.onesr_b[:], start=False, stop=True)
            s_gt.inc(t.last)
        self.on("tensor", f_gsc)

        def f_gev(v):
            s_gt.wait(v, 1)
            i = v.tensor_copy(Ff[0:8, 0:2048], P0[0:8, :])
            s_gt.inc(i)
        self.on("vector", f_gev)

        def f_gtr(t):
            s_gt.wait(t, 2)
            for c in range(16):
                ps = P1 if c % 2 == 0 else P2
                if c >= 2:
                    s_gr.wait(t, c - 1)
                i = t.transpose(ps[:, 0:8], Ff[0:8, 128 * c:128 * (c + 1)],
                                self.eye_f[0:8, 0:8])
                s_gt.inc(i)
        self.on("tensor", f_gtr)

        def f_gcp(v):
            for c in range(16):
                ps = P1 if c % 2 == 0 else P2
                s_gt.wait(v, 3 + c)
                i = v.tensor_copy(self.gTM[:, 8 * c:8 * (c + 1)], ps[:, 0:8])
                s_gr.inc(i)
        self.on("vector", f_gcp)

        s_gs = self.sem("gs")
        nm_col = Ff[:, 3072:3073]
        sm_col = Ff[:, 3073:3074]

        def f_gsm(v):
            for c in range(16):
                g = self.gTM[:, 8 * c:8 * (c + 1)]
                s_gr.wait(v, c + 1)
                if c > 0:
                    s_gs.wait(v, 2 * c)
                i = v.reduce_max(nm_col, g, axis=AX.X, negate=True)
                s_gs.inc(i)
        self.on("vector", f_gsm)

        def f_gse(s):
            for c in range(16):
                g = self.gTM[:, 8 * c:8 * (c + 1)]
                s_gs.wait(s, 2 * c + 1)
                i = s.activation(g, g, AF.Exp, bias=nm_col)
                s_gs.inc(i)
        self.on("scalar", f_gse)

        s_gf = self.sem("gf")

        s_gs2 = self.sem("gs2")

        def f_gsm2(v):
            for c in range(16):
                g = self.gTM[:, 8 * c:8 * (c + 1)]
                s_gs.wait(v, 2 * c + 2)
                i = v.reduce_sum(sm_col, g, axis=AX.X)
                s_gs2.inc(i)
                s_gs2.wait(v, 3 * c + 1)
                i = v.reciprocal(sm_col, sm_col)
                s_gs2.inc(i)
                s_gs2.wait(v, 3 * c + 2)
                i = v.tensor_scalar(g, g, sm_col, None, ALU.mult)
                s_gs2.inc(i)
                s_gf.inc(v.engine_nop()) if False else None
                s_gf.inc(i) if False else None
        self.on("vector", f_gsm2)
        s_gf.n = s_gf.base

        s_gb = self.sem("gb")
        s_gbc = self.sem("gbc")

        def f_gbt(t):
            for c in range(16):
                ps = P1 if c % 2 == 0 else P2
                s_gs2.wait(t, 3 * c + 3)
                if c >= 2:
                    s_gbc.wait(t, c - 1)
                i = t.transpose(ps[0:8, 0:128], self.gTM[:, 8 * c:8 * (c + 1)],
                                self.eye_f[:])
                s_gb.inc(i)
        self.on("tensor", f_gbt)

        def f_gbc(v):
            for c in range(16):
                ps = P1 if c % 2 == 0 else P2
                s_gb.wait(v, c + 1)
                i = v.tensor_copy(self.Fb[0:8, 4096 + 128 * c:4096 + 128 * (c + 1)],
                                  ps[0:8, 0:128])
                s_gbc.inc(i)
        self.on("vector", f_gbc)
        self.done(s_gbc); self.done(s_gs2)
        self.bar()
        self.dump_sbuf(f"dbg_g{li}", self.gTM[:], [128, 128], F32)

        self.scope("xp")
        s_xp = self.sem("xp", "dmah")
        s_xpd = self.sem("xpd")
        xch = Ff[:, 0:2048]

        def f_xp(sync):
            for fc in range(NFC):
                if fc > 0:
                    s_xpd.wait(sync, fc)
                i = sync.dma_start(out=xch, in_=self.spx[:, 2048 * fc:2048 * (fc + 1)])
                s_xp.inc(i, 16)
                s_xpd.wait(sync, fc + 1)
                i = sync.dma_start(out=self.spx[:, 2048 * fc:2048 * (fc + 1)],
                                   in_=A[:, 2048 * fc:2048 * (fc + 1)])
                s_xp.inc(i, 16)
        self.on("sync", f_xp)

        def f_xpd(v):
            for fc in range(NFC):
                s_xp.wait(v, 32 * fc + 16)
                uc = A[:, 2048 * fc:2048 * (fc + 1)]
                i = v.scalar_tensor_tensor(uc, uc, 0.5, xch, ALU.mult, ALU.add)
                s_xpd.inc(i, 1)
        self.on("vector", f_xpd)
        self.done(s_xp); self.done(s_xpd)
        self.bar()

        self.scope("moe")
        self.emit_moe(li, last)

        if self.dbg and not last:
            self.dump_dram(f"dbg_out{li}", self.spx[:], [128, 8 * TOK], F32)
            self.bar()
        if not last:
            s_xr = self.sem("xr", "dmah")

            def f_xr(sync):
                for fc in range(NFC):
                    i = sync.dma_start(out=A[:, 2048 * fc:2048 * (fc + 1)],
                                       in_=self.spx[:, 2048 * fc:2048 * (fc + 1)])
                    s_xr.inc(i, 16)
            self.on("sync", f_xr)
            self.done(s_xr)
            self.bar()
        self.release_scope()

    def emit_moe(self, li, last):
        P0, P1, P2 = self.P0, self.P1, self.P2
        A, Ab, Bf, Cb, Db, Eb, Ff, Fb = (self.A, self.Ab, self.Bf, self.Cb,
                                         self.Db, self.Eb, self.Ff, self.Fb)
        s_w1 = self.sem("w1", "dmas")
        s_w2 = self.sem("w2", "dmas")
        s_hp = self.sem("hp")
        s_he = self.sem("he")
        s_b2 = self.sem("b2", "dmah")
        s_ymm = self.sem("ymm")
        s_yd = self.sem("yd")
        s_p7p = self.sem("p7p")
        s_p7t = self.sem("p7t")
        s_p7d = self.sem("p7d")
        s_p7x = self.sem("p7x", "dmah")
        E1 = [Eb[:, 0:4096], Eb[:, 4096:8192]]
        C1 = [Cb[:, 0:8192], Cb[:, 8192:16384]]
        b2row = Fb[0:1, 6144:7168]
        gFM = Fb[0:8, 4096:6144]

        def f_w12(g):
            def w1_eighth(te, ei):
                e = te % NE
                gi = 8 * te + ei
                if gi >= 2:
                    s_hp.wait(g, 4 * (gi - 1))
                srcw = self.fW1[e].rearrange("(c p) d -> p c d", p=128)
                i = g.dma_start(out=E1[gi % 2].rearrange("p (c n) -> p c n", n=512),
                                in_=srcw[:, :, 512 * ei:512 * (ei + 1)])
                s_w1.inc(i, 16)

            for ei in range(8):
                w1_eighth(0, ei)
            for tb in range(2):
                for e in range(NE):
                    te = 8 * tb + e
                    for fq in range(4):
                        gq = 4 * te + fq
                        if gq >= 2:
                            s_ymm.wait(g, 8 * (gq - 1))
                        srcw = self.fW2[e].rearrange("(c p) f -> p c f", p=128)
                        i = g.dma_start(out=C1[gq % 2].rearrange("p (c n) -> p c n", n=256),
                                        in_=srcw[:, :, 256 * fq:256 * (fq + 1)])
                        s_w2.inc(i, 16)
                    if te + 1 < 16:
                        for ei in range(8):
                            w1_eighth(te + 1, ei)
        self.on("gpsimd", f_w12)



        gFM_t = self.Fb[0:8, 4096:6144]

        def p7_pe(t, tb):
            s_yd.wait(t, 256 * (tb + 1))
            for fc in range(NFC):
                j = 8 * tb + fc
                if j > 0:
                    s_p7d.wait(t, 9 * j)
                for sl in range(2):
                    t.matmul(P0[:, 512 * sl:512 * (sl + 1)],
                             self.b2sb[:, 128 * fc:128 * (fc + 1)],
                             gFM_t[:, 1024 * tb + 512 * sl:1024 * tb + 512 * (sl + 1)],
                             start=True, stop=True)
                s_p7p.inc(t.last)
                for tt in range(8):
                    ps = P1 if tt % 2 == 0 else P2
                    if tt >= 2:
                        s_p7d.wait(t, 9 * j + tt)
                    i = t.transpose(ps[:, 0:128],
                                    self.Bf[:, 1024 * tt + 128 * fc:1024 * tt + 128 * (fc + 1)],
                                    self.eye_f[:])
                    s_p7t.inc(i)

        def p7_dve(v, tb):
            for fc in range(NFC):
                j = 8 * tb + fc
                fp = Ff[:, 1024 * (fc % 2):1024 * (fc % 2 + 1)]
                s_p7p.wait(v, j + 1)
                s_p7x.wait(v, 16 * (j + 1))
                i = v.scalar_tensor_tensor(fp, P0[:, 0:1024], 0.5, fp, ALU.mult, ALU.add)
                s_p7d.inc(i)
                for tt in range(8):
                    ps = P1 if tt % 2 == 0 else P2
                    s_p7t.wait(v, 8 * j + tt + 1)
                    sl = fp[:, 128 * tt:128 * (tt + 1)]
                    i = v.scalar_tensor_tensor(sl, ps[:, 0:128], 0.5, sl, ALU.mult, ALU.add)
                    s_p7d.inc(i)

        def f_hmm(t):
            for tb in range(2):
                for e in range(NE):
                    te = 8 * tb + e
                    if tb == 1 and e == 0:
                        s_p7d.wait(t, 72)
                    for dc in range(32):
                        gd = 32 * te + dc
                        ei = dc // 4
                        gi = 8 * te + ei
                        s_w1.wait(t, 16 * (gi + 1))
                        if gd >= 2:
                            s_he.wait(t, gd - 1)
                        half = P0[:, 0:1024] if dc % 2 == 0 else P0[:, 1024:2048]
                        for sl in range(2):
                            for ic in range(NFC):
                                t.matmul(half[:, 512 * sl:512 * (sl + 1)],
                                         E1[gi % 2][:, 512 * ic + 128 * (dc % 4):512 * ic + 128 * (dc % 4) + 128],
                                         Db[:, 2048 * ic + 1024 * tb + 512 * sl:2048 * ic + 1024 * tb + 512 * (sl + 1)],
                                         start=(ic == 0), stop=(ic == NFC - 1))
                        s_hp.inc(t.last)
                    s_he.wait(t, 32 * te + 32)
                    s_b2.wait(t, 16 * (te + 1))
                    for fq in range(4):
                        gq = 4 * te + fq
                        s_w2.wait(t, 16 * (gq + 1))
                        for tt in range(8):
                            gy = 8 * gq + tt
                            ps = P1 if tt % 2 == 0 else P2
                            if gy >= 2:
                                s_yd.wait(t, gy - 1)
                            for dc in range(32):
                                t.matmul(ps[:, 0:256],
                                         Ab[:, 1024 * dc + 128 * tt:1024 * dc + 128 * (tt + 1)],
                                         C1[gq % 2][:, 256 * dc:256 * (dc + 1)],
                                         start=(dc == 0), stop=(dc == 31))
                            s_ymm.inc(t.last)
                p7_pe(t, tb)
        self.on("tensor", f_hmm)

        def f_he(s):
            for tb in range(2):
                for e in range(NE):
                    te = 8 * tb + e
                    for dc in range(32):
                        gd = 32 * te + dc
                        s_hp.wait(s, gd + 1)
                        if te > 0 and dc == 0:
                            s_ymm.wait(s, 32 * te)
                        half = P0[:, 0:1024] if dc % 2 == 0 else P0[:, 1024:2048]
                        i = s.activation(Ab[:, 1024 * dc:1024 * (dc + 1)], half,
                                         AF.Relu, bias=self.b1T[:, 32 * e + dc:32 * e + dc + 1])
                        s_he.inc(i)
        self.on("scalar", f_he)

        def f_yd(v):
            for tb in range(2):
                if tb == 1:
                    s_p7d.wait(v, 72)
                for e in range(NE):
                    te = 8 * tb + e
                    for fq in range(4):
                        gq = 4 * te + fq
                        for tt in range(8):
                            gy = 8 * gq + tt
                            ps = P1 if tt % 2 == 0 else P2
                            s_ymm.wait(v, gy + 1)
                            dst = Bf[:, 1024 * tt + 256 * fq:1024 * tt + 256 * (fq + 1)]
                            c0 = 8 * (8 * tb + tt) + e
                            gcol = self.gTM[:, c0:c0 + 1]
                            if e == 0:
                                i = v.tensor_scalar(dst, ps[:, 0:256], gcol, None, ALU.mult)
                            else:
                                i = v.scalar_tensor_tensor(dst, ps[:, 0:256], gcol, dst,
                                                           ALU.mult, ALU.add)
                            s_yd.inc(i)
                p7_dve(v, tb)
        self.on("vector", f_yd)

        st0 = self.s_st.n

        def f_p7sync(sync):
            for tb in range(2):
                for e in range(NE):
                    te = 8 * tb + e
                    if te > 0:
                        s_ymm.wait(sync, 32 * te)
                    i = sync.dma_start(out=b2row, in_=self.b2sb[e:e + 1, :])
                    s_b2.inc(i, 16)
                for fc in range(NFC):
                    j = 8 * tb + fc
                    if j >= 2:
                        self.s_st.wait_abs(sync, st0 + 16 * (j - 1))
                    i = sync.dma_start(out=Ff[:, 1024 * (fc % 2):1024 * (fc % 2 + 1)],
                                       in_=self.spx[:, 2048 * fc + 1024 * tb:2048 * fc + 1024 * tb + 1024])
                    s_p7x.inc(i, 16)
                    s_p7d.wait(sync, 9 * j + 9)
                    if last:
                        dst = self.yT.rearrange("(c p) t -> p c t", p=128)[:, fc, 1024 * tb:1024 * tb + 1024]
                    else:
                        dst = self.spx[:, 2048 * fc + 1024 * tb:2048 * fc + 1024 * tb + 1024]
                    i = sync.dma_start(out=dst, in_=Ff[:, 1024 * (fc % 2):1024 * (fc % 2 + 1)])
                    self.s_st.inc(i, 16)
        self.on("sync", f_p7sync)
        s_p7d.n = s_p7d.base + 144

        self.done(self.s_st); self.done(s_p7d); self.done(s_yd); self.done(s_he)
        self.done(s_w1); self.done(s_w2); self.done(s_b2); self.done(s_p7x)
        self.bar()


    def emit_attn(self, last):
        P0, P1, P2 = self.P0, self.P1, self.P2
        A, Bb, B65, Cb, C65 = self.A, self.Bb, self.B65, self.Cb, self.C65
        Db, Df, Eb, Ff, Fb = self.Db, self.Df, self.Eb, self.Ff, self.Fb

        self.scope("aln")
        s_lnp = self.emit_ln("double")
        lnp_n = s_lnp.n - s_lnp.base
        stage = Ff[:, 2048:4096]
        s_app = self.sem("app")
        s_spl = self.sem("spl", "dmah")

        def f_app(v):
            s_lnp.wait(v, lnp_n)
            for fc in range(NFC):
                xc = A[:, 2048 * fc:2048 * (fc + 1)]
                if fc > 0:
                    s_spl.wait(v, 16 * fc)
                v.tensor_sub(stage, xc, P0[:])
                s_app.inc(v.last)
                for qh in range(2):
                    ps = P1 if qh == 0 else P2
                    v.tensor_mul(stage[:, 1024 * qh:1024 * (qh + 1)],
                                 stage[:, 1024 * qh:1024 * (qh + 1)], ps[:])
                i = v.tensor_scalar(Bb[:, 2048 * fc:2048 * (fc + 1)], stage,
                                    self.anw_c[:, fc:fc + 1], self.anb_c[:, fc:fc + 1],
                                    ALU.mult, ALU.add)
                s_app.inc(i)
        self.on("vector", f_app)

        def f_spl(sync):
            for fc in range(NFC):
                s_app.wait(sync, 2 * fc + 1)
                s_spl.inc(sync.dma_start(out=self.spx[:, 2048 * fc:2048 * (fc + 1)],
                                         in_=A[:, 2048 * fc:2048 * (fc + 1)]), 16)
        self.on("sync", f_spl)
        self.done(s_app); self.done(s_spl)
        self.bar()

        if self.attn_trunc(1):
            return
        s_wl = self.sem("wl", "dmas")

        def f_wdma(g):
            i = g.dma_start(out=Eb[:, 0:4096].rearrange("p (c n) -> p c n", n=512),
                            in_=self.aWq.rearrange("(c p) n -> p c n", p=128))
            s_wl.inc(i, 16)
            i = g.dma_start(out=Eb[:, 4096:5120].rearrange("p (c n) -> p c n", n=128),
                            in_=self.aWkv.rearrange("(c p) n -> p c n", p=128))
            s_wl.inc(i, 16)
        self.on("gpsimd", f_wdma)

        self.scope("aproj")
        s_qp = self.sem("qp")
        s_qe = self.sem("qe")

        def f_q(t):
            s_wl.wait(t, 16)
            for h in range(8):
                if h > 0:
                    s_qe.wait(t, h)
                for sl in range(4):
                    for ic in range(NFC):
                        t.matmul(P0[0:64, 512 * sl:512 * (sl + 1)],
                                 Eb[:, 512 * ic + 64 * h:512 * ic + 64 * h + 64],
                                 Bb[:, 2048 * ic + 512 * sl:2048 * ic + 512 * (sl + 1)],
                                 start=(ic == 0), stop=(ic == NFC - 1))
                s_qp.inc(t.last)
        self.on("tensor", f_q)

        def f_qe(s):
            for h in range(8):
                s_qp.wait(s, h + 1)
                i = s.activation(C65[0:64, 2048 * h:2048 * h + 2048],
                                 P0[0:64, :], AF.Copy)
                s_qe.inc(i)
        self.on("scalar", f_qe)

        s_kp = self.sem("kp")
        s_ke = self.sem("ke")
        s_ks = self.sem("ks", "dmah")
        s_kn = self.sem("kn")
        kstage = Fb[0:64, 0:2048]
        ksq = Fb[0:64, 2048:4096]
        vf32 = Ff[0:64, 2048:4096]

        def f_kv(t):
            s_wl.wait(t, 32)
            s_qe.wait(t, 8)
            for sl in range(4):
                for ic in range(NFC):
                    t.matmul(P0[0:64, 512 * sl:512 * (sl + 1)],
                             Eb[:, 4096 + 128 * ic:4096 + 128 * ic + 64],
                             Bb[:, 2048 * ic + 512 * sl:2048 * ic + 512 * (sl + 1)],
                             start=(ic == 0), stop=(ic == NFC - 1))
            s_kp.inc(t.last)
            for sl in range(4):
                ps = P1 if sl < 2 else P2
                for ic in range(NFC):
                    t.matmul(ps[0:64, 512 * (sl % 2):512 * (sl % 2 + 1)],
                             Eb[:, 4096 + 128 * ic + 64:4096 + 128 * ic + 128],
                             Bb[:, 2048 * ic + 512 * sl:2048 * ic + 512 * (sl + 1)],
                             start=(ic == 0), stop=(ic == NFC - 1))
            s_kp.inc(t.last)
        self.on("tensor", f_kv)

        def f_kve(s):
            s_kp.wait(s, 1)
            i = s.activation(kstage, P0[0:64, :], AF.Copy)
            s_ke.inc(i)
            i = s.activation(ksq, kstage, AF.Square)
            s_ke.inc(i)
        self.on("scalar", f_kve)

        def f_kvv(v):
            s_kp.wait(v, 2)
            v.tensor_copy(vf32[:, 0:1024], P1[0:64, :])
            i = v.tensor_copy(vf32[:, 1024:2048], P2[0:64, :])
            s_ke.inc(i)
        self.on("vector", f_kvv)

        def f_ksdma(sync):
            s_ke.wait(sync, 1)
            i = sync.dma_start(out=self.kvsl_a[0:64, :], in_=kstage)
            s_ks.inc(i, 16)
        self.on("sync", f_ksdma)

        def f_knt(t):
            s_ke.wait(t, 3)
            for qh in range(2):
                ps = P1 if qh == 0 else P2
                for sl in range(2):
                    c0 = 1024 * qh + 512 * sl
                    t.matmul(ps[0:1, 512 * sl:512 * (sl + 1)], self.ones_b[0:64, :],
                             ksq[:, c0:c0 + 512], start=True, stop=True,
                             skip_group_check=True)
            s_kn.inc(t.last)
        self.on("tensor", f_knt)

        def f_knv(v):
            s_kn.wait(v, 1)
            v.tensor_copy(self.negC[:, 0:1024], P1[0:1, :])
            i = v.tensor_copy(self.negC[:, 1024:2048], P2[0:1, :])
            s_kn.inc(i)
            s_kn.wait(v, 2)
            i = v.reduce_max(self.Mrow[:, 0:1], self.negC[:], axis=AX.X)
            s_kn.inc(i)
        self.on("vector", f_knv)

        s_vtt = self.sem("vtt")
        s_vt = self.sem("vt")
        s_vts = self.sem("vts", "dmah")
        vtst = Fb[:, 2048:2176]

        def f_vt(t):
            s_kn.wait(t, 2)
            for tt in range(16):
                ps = P1 if tt % 2 == 0 else P2
                if tt >= 2:
                    s_vt.wait(t, tt - 1)
                i = t.transpose(ps[:, 0:64], vf32[:, 128 * tt:128 * (tt + 1)],
                                self.eye_f[0:64, 0:64])
                s_vtt.inc(i)
        self.on("tensor", f_vt)

        def f_vte(v):
            for tt in range(16):
                ps = P1 if tt % 2 == 0 else P2
                s_vtt.wait(v, tt + 1)
                if tt >= 2:
                    s_vts.wait(v, 16 * (tt - 1))
                i = v.tensor_copy(vtst[:, 64 * (tt % 2):64 * (tt % 2) + 64], ps[:, 0:64])
                s_vt.inc(i)
        self.on("vector", f_vte)

        def f_vts(sync):
            for tt in range(16):
                s_vt.wait(sync, tt + 1)
                dst = self.kvsl_a[64 + 4 * tt:64 + 4 * (tt + 1), :]
                i = sync.dma_start(
                    out=dst.rearrange("a n -> (a n)").rearrange("(p f) -> p f", f=64),
                    in_=vtst[:, 64 * (tt % 2):64 * (tt % 2) + 64])
                s_vts.inc(i, 16)
        self.on("sync", f_vts)
        if self.dbg:
            self.done(s_vts); self.done(s_qe); self.done(s_ke)
            self.bar()
            self.dump_sbuf("dbg_axn", Bb[:], [128, 8 * TOK], BF16)

        if self.attn_trunc(2):
            return
        s_cc = self.sem("cc", "cc")
        ks_n, vts_n = s_ks.n - s_ks.base, 16 * 16

        def f_cc(g):
            s_ks.wait(g, ks_n)
            s_vts.wait(g, vts_n)
            i = g.collective_compute("AllGather", ALU.bypass,
                                     replica_groups=[list(range(NCORES))],
                                     ins=[self.kvsl_a[:]], outs=[self.kvg_a[:]])
            s_cc.inc(i)
        self.on("gpsimd", f_cc)
        cc_n = s_cc.n - s_cc.base

        s_kl = self.sem("kl", "dmah")

        def f_kl(sync):
            s_cc.wait(sync, cc_n)
            for r in range(NCORES):
                i = sync.dma_start(out=B65[0:64, 2048 * r:2048 * (r + 1)],
                                   in_=self.kvg_a[KV_ROWS_A * r:KV_ROWS_A * r + 64, :])
                s_kl.inc(i, 16)
            for kt in range(128):
                r, ktl = kt // 16, kt % 16
                vr0 = KV_ROWS_A * r + 64 + 4 * ktl
                src = self.kvg_a[vr0:vr0 + 4, :]
                i = sync.dma_start(
                    out=Db[:, 65 * kt:65 * kt + 64],
                    in_=src.rearrange("a n -> (a n)").rearrange("(p f) -> p f", f=64))
                s_kl.inc(i, 16)
        self.on("sync", f_kl)
        kl_n = s_kl.n - s_kl.base

        s_kf = self.sem("kf")

        def f_kfill(g):
            s_cc.wait(g, cc_n)
            g.memset(B65[64:65, :], 1.0)
            i = g.memset(Db[:, 0:8320].rearrange("p (t c) -> p t c", c=65)[:, :, 64:65], 1.0)
            s_kf.inc(i, 1)
        self.on("gpsimd", f_kfill)

        s_m = self.sem("m")
        self.done(s_m); self.done(s_kl); self.done(s_kf); self.done(s_qe)
        self.bar()
        if self.attn_trunc(3):
            return
        self.dump_dram("dbg_akvg", self.kvg_a[:], [NCORES * KV_ROWS_A, TOK], BF16)

        self.scope("aqn")
        qsq = Fb[:, 0:2048]
        s_sq2 = self.sem("sq2")
        s_qnm = self.sem("qnm")
        s_ngc = self.sem("ngc")
        s_asc = self.sem("asc")
        s_aex = self.sem("aex")
        s_aav = self.sem("aav")
        s_anm = self.sem("anm")

        def f_hsq(s):
            for h in range(8):
                if h > 0:
                    s_qnm.wait(s, h)
                i = s.activation(qsq[0:64, :], C65[0:64, 2048 * h:2048 * (h + 1)],
                                 AF.Square)
                s_sq2.inc(i)
                s_ngc.wait(s, 4 * h + 2)
                i = s.activation(self.negC[:], self.negC[:], AF.Sqrt)
                s_ngc.inc(i)
        self.on("scalar", f_hsq)

        def f_qnt(t):
            for h in range(8):
                s_sq2.wait(t, h + 1)
                if h > 0:
                    s_ngc.wait(t, 4 * h - 3)
                for qh in range(2):
                    ps = P1 if qh == 0 else P2
                    for sl in range(2):
                        c0 = 1024 * qh + 512 * sl
                        t.matmul(ps[0:1, 512 * sl:512 * (sl + 1)], self.ones_b[0:64, :],
                                 qsq[0:64, c0:c0 + 512], start=True, stop=True,
                                 skip_group_check=True)
                s_qnm.inc(t.last)
        self.on("tensor", f_qnt)

        def f_qnv(v):
            for h in range(8):
                s_qnm.wait(v, h + 1)
                v.tensor_copy(self.negC[:, 0:1024], P1[0:1, :])
                i = v.tensor_copy(self.negC[:, 1024:2048], P2[0:1, :])
                s_ngc.inc(i)
                i = v.tensor_scalar(self.negC[:], self.negC[:], self.Mrow[:, 0:1],
                                    None, ALU.mult)
                s_ngc.inc(i)
                s_ngc.wait(v, 4 * h + 3)
                i = v.tensor_scalar(C65[64:65, 2048 * h:2048 * (h + 1)], self.negC[:],
                                    -1.0, None, ALU.mult)
                s_ngc.inc(i)
        self.on("vector", f_qnv)

        self.done(s_ngc); self.done(s_qnm)
        self.bar()
        if self.attn_trunc(4):
            return
        self.dump_sbuf("dbg_aq", C65[:], [65, 8 * TOK], BF16)

        self.scope("aattn")
        srow_a = Ff[64:65, 0:1024]
        apre = Fb[0:64, 2048:3072]
        apre2 = Fb[0:64, 3072:4096]
        s_adm = self.sem("adm", "dmah")

        def f_att(t):
            T3 = [P0[:, 0:1024], P0[:, 1024:2048], P2[:, 0:1024]]
            for h in range(8):
                hq2 = C65[:, 2048 * h:2048 * (h + 1)]
                for qh in range(2):
                    hq = 2 * h + qh
                    base = 128 * hq

                    def av(t, j, hq=hq, base=base):
                        ep = Eb[:, 1024 * (j % 4):1024 * (j % 4) + 1024]
                        for sl in range(2):
                            t.matmul(P1[0:65, 512 * sl:512 * (sl + 1)],
                                     Db[:, 65 * j:65 * (j + 1)],
                                     ep[:, 512 * sl:512 * (sl + 1)],
                                     start=(j == 0), stop=(j == 127))
                        s_aav.inc(t.last)

                    for kt in range(128):
                        g = base + kt
                        if g >= 3:
                            s_aex.wait(t, g - 2)
                        tile = T3[g % 3]
                        for sl in range(2):
                            t.matmul(tile[:, 512 * sl:512 * (sl + 1)],
                                     B65[:, 128 * kt:128 * (kt + 1)],
                                     hq2[:, 1024 * qh + 512 * sl:1024 * qh + 512 * (sl + 1)],
                                     start=True, stop=True)
                        s_asc.inc(t.last)
                        if kt >= 3:
                            if kt == 3:
                                s_anm.wait(t, 4 * hq)
                            av(t, kt - 3)
                    for j in (125, 126, 127):
                        s_aex.wait(t, base + j + 1)
                        av(t, j)
                    s_anm.wait(t, 4 * hq + 3)
                    for sl in range(2):
                        t.matmul(P1[0:64, 512 * sl:512 * (sl + 1)],
                                 self.ones2d[64:65, :],
                                 srow_a[:, 512 * sl:512 * (sl + 1)],
                                 start=True, stop=True)
                    s_aav.inc(t.last)
        self.on("tensor", f_att)

        def f_exp(s):
            for h in range(8):
                for qh in range(2):
                    hq = 2 * h + qh
                    base = 128 * hq
                    T3 = [P0[:, 0:1024], P0[:, 1024:2048], P2[:, 0:1024]]
                    for kt in range(128):
                        g = base + kt
                        half = T3[g % 3]
                        s_asc.wait(s, g + 1)
                        if kt >= 4:
                            s_aav.wait(s, 129 * hq + kt - 3)
                        elif hq > 0:
                            s_aav.wait(s, 129 * hq - 1)
                        ep = Eb[:, 1024 * (kt % 4):1024 * (kt % 4) + 1024]
                        i = s.activation(ep, half, AF.Exp, scale=SCL_A)
                        s_aex.inc(i)
        self.on("scalar", f_exp)

        def f_nrm(v):
            for h in range(8):
                for qh in range(2):
                    hq = 2 * h + qh
                    s_aav.wait(v, 129 * hq + 128)
                    i = v.tensor_copy(srow_a, P1[64:65, 0:1024])
                    s_anm.inc(i)
                    i = v.tensor_copy(apre, P1[0:64, 0:1024])
                    s_anm.inc(i)
                    i = v.reciprocal(srow_a, srow_a)
                    s_anm.inc(i)
                    s_aav.wait(v, 129 * hq + 129)
                    c = h // 2
                    if h % 2 == 0:
                        dst = Db[0:64, 8320 + 2048 * c + 1024 * qh:8320 + 2048 * c + 1024 * (qh + 1)]
                        i = v.tensor_mul(dst, apre, P1[0:64, 0:1024])
                    else:
                        odd_idx = (hq - 2) // 4 * 2 + qh
                        odd_n = h // 2 * 2 + qh
                        if odd_n > 0:
                            s_adm.wait(v, 16 * odd_n)
                        i = v.tensor_mul(apre2, apre, P1[0:64, 0:1024])
                    s_anm.inc(i)
        self.on("vector", f_nrm)

        def f_adm(sync):
            for h in range(1, 8, 2):
                for qh in range(2):
                    hq = 2 * h + qh
                    c = h // 2
                    s_anm.wait(sync, 4 * hq + 4)
                    dst = Db[64:128, 8320 + 2048 * c + 1024 * qh:8320 + 2048 * c + 1024 * (qh + 1)]
                    i = sync.dma_start(out=dst, in_=apre2)
                    s_adm.inc(i, 16)
        self.on("sync", f_adm)
        self.done(s_anm); self.done(s_adm)
        self.bar()
        if self.attn_trunc(5):
            return
        self.dump_sbuf("dbg_aall", Db[:, 8320:16512], [128, 8192], BF16)

        self.scope("awo")
        s_wo = self.sem("wo", "dmas")

        def f_wodma(g):
            s_kp.wait(g, 2)
            i = g.dma_start(out=Eb[:, 4096:8192].rearrange("p (c n) -> p c n", n=1024),
                            in_=self.aWo.rearrange("(c p) n -> p c n", p=128))
            s_wo.inc(i, 16)
        self.on("gpsimd", f_wodma)

        s_op = self.sem("op")
        s_oe = self.sem("oe")
        s_os = self.sem("os")

        def f_wo(t):
            s_wo.wait(t, 16)
            for oc in range(NFC):
                if oc > 0:
                    s_oe.wait(t, 3 * oc)
                for sl in range(4):
                    for ic in range(4):
                        t.matmul(P0[:, 512 * sl:512 * (sl + 1)],
                                 Eb[:, 4096 + 1024 * ic + 128 * oc:4096 + 1024 * ic + 128 * oc + 128],
                                 Db[:, 8320 + 2048 * ic + 512 * sl:8320 + 2048 * ic + 512 * (sl + 1)],
                                 start=(ic == 0), stop=(ic == 3))
                s_op.inc(t.last)
                s_oe.wait(t, 3 * oc + 3)
                for qt in range(16):
                    first = (oc == 0 and qt == 0)
                    lastm = (oc == NFC - 1 and qt == 15)
                    t.matmul(P1[:, qt:qt + 1], Fb[:, 128 * qt:128 * (qt + 1)],
                             self.ones_b[:], start=first, stop=False,
                             skip_group_check=True)
                    t.matmul(P1[:, 16 + qt:16 + qt + 1],
                             Fb[:, 2048 + 128 * qt:2048 + 128 * (qt + 1)],
                             self.ones_b[:], start=False, stop=lastm,
                             skip_group_check=True)
                s_op.inc(t.last)
        self.on("tensor", f_wo)

        def f_woe(s):
            for oc in range(NFC):
                s_op.wait(s, 2 * oc + 1)
                i = s.activation(Fb[:, 0:2048], P0[:], AF.Copy)
                s_oe.inc(i)
                i = s.activation(Fb[:, 2048:4096], Fb[:, 0:2048], AF.Square)
                s_oe.inc(i)
        self.on("scalar", f_woe)

        def f_wov(v):
            for oc in range(NFC):
                s_op.wait(v, 2 * oc + 1)
                s_oe.wait(v, 3 * oc + 1)
                i = v.tensor_copy(A[:, 2048 * oc:2048 * (oc + 1)], P0[:])
                s_oe.inc(i)
        self.on("vector", f_wov)
        self.done(s_oe); self.done(s_op)
        self.bar()
        if self.attn_trunc(6):
            return
        self.dump_sbuf("dbg_ao", A[:], [128, 8 * TOK], F32)

        s_or = self.sem("or")
        s_orD = self.sem("orD", "dmah")

        def f_oev(v):
            i = v.tensor_copy(Ff[:, 0:32], P1[:, 0:32])
            s_or.inc(i, 1)
        self.on("vector", f_oev)

        for stat in range(2):
            def f_tt(t, stat=stat):
                s_or.wait(t, 1 + 2 * stat)
                i = t.transpose(P2[0:16, 0:128], Ff[:, 16 * stat:16 * (stat + 1)],
                                self.eye_f[:])
                s_or.inc(i, 1)
            self.on("tensor", f_tt)

            def f_tv(v, stat=stat):
                s_or.wait(v, 2 + 2 * stat)
                if stat >= 1:
                    s_orD.wait(v, 32 * stat - 16)
                i = v.tensor_copy(self.tr16[:], P2[0:16, 0:128])
                s_or.inc(i, 1)
            self.on("vector", f_tv)

            def f_ts(sync, stat=stat):
                s_or.wait(sync, 3 + 2 * stat)
                i = sync.dma_start(out=self.rowb[:], in_=self.tr16[:])
                s_orD.inc(i, 16)
                s_orD.wait(sync, 32 * stat + 16)
                i = sync.dma_start(out=self.Drows[:, 2048 * stat:2048 * (stat + 1)],
                                   in_=self.rowb.rearrange("a b -> (a b)")[None, :])
                s_orD.inc(i, 16)
            self.on("sync", f_ts)

        mu_a = self.Drows[:, 0:2048]
        sq_a = self.Drows[:, 2048:4096]
        sr_a = self.Drows[:, 4096:6144]
        s_orr = self.sem("orr")

        def f_orow(v):
            s_or.wait(v, 5)
            s_orD.wait(v, 64)
            v.tensor_scalar(mu_a, mu_a, 1.0 / 1024.0, None, ALU.mult)
            v.tensor_scalar(sq_a, sq_a, 1.0 / 1024.0, None, ALU.mult)
            v.tensor_mul(sr_a, mu_a, mu_a)
            i = v.tensor_sub(sr_a, sq_a, sr_a)
            s_orr.inc(i)
        self.on("vector", f_orow)

        def f_oract(s):
            s_orr.wait(s, 1)
            i = s.activation(sr_a, sr_a, AF.Sqrt, bias=self.eps_c[0:1, :])
            s_orr.inc(i)
        self.on("scalar", f_oract)

        def f_orr(v):
            s_orr.wait(v, 2)
            i = v.reciprocal(sr_a, sr_a)
            s_orr.inc(i)
        self.on("vector", f_orr)

        def f_obc(t):
            s_orr.wait(t, 3)
            for sl in range(4):
                t.matmul(P0[:, 512 * sl:512 * (sl + 1)], self.onesr_f[:],
                         mu_a[:, 512 * sl:512 * (sl + 1)], start=True, stop=True)
            for sl in range(2):
                t.matmul(P1[:, 512 * sl:512 * (sl + 1)], self.onesr_f[:],
                         sr_a[:, 512 * sl:512 * (sl + 1)], start=True, stop=True)
                t.matmul(P2[:, 512 * sl:512 * (sl + 1)], self.onesr_f[:],
                         sr_a[:, 1024 + 512 * sl:1024 + 512 * (sl + 1)], start=True, stop=True)
            s_orr.inc(t.last)
        self.on("tensor", f_obc)
        orr_n = s_orr.n - s_orr.base

        self.scope("aapply")
        s_fx = self.sem("fx")
        s_fxl = self.sem("fxl", "dmah")
        ast = Df[:, 0:2048]
        xres = Ff[:, 0:2048]

        def f_fxs(sync):
            for fc in range(NFC):
                if fc > 0:
                    s_fx.wait(sync, 2 * fc)
                i = sync.dma_start(out=xres, in_=self.spx[:, 2048 * fc:2048 * (fc + 1)])
                s_fxl.inc(i, 16)
        self.on("sync", f_fxs)

        def f_fx(v):
            s_orr.wait(v, orr_n)
            for fc in range(NFC):
                ac = A[:, 2048 * fc:2048 * (fc + 1)]
                s_fxl.wait(v, 16 * fc + 16)
                v.tensor_sub(ast, ac, P0[:])
                for qh in range(2):
                    ps = P1 if qh == 0 else P2
                    v.tensor_mul(ast[:, 1024 * qh:1024 * (qh + 1)],
                                 ast[:, 1024 * qh:1024 * (qh + 1)], ps[:])
                i = v.tensor_scalar(ast, ast, self.aonw_c[:, fc:fc + 1],
                                    self.aonb_c[:, fc:fc + 1], ALU.mult, ALU.add)
                s_fx.inc(i, 1)
                i = v.tensor_add(ac, ast, xres)
                s_fx.inc(i, 1)
        self.on("vector", f_fx)
        self.done(s_fx); self.done(s_fxl)

        if last:
            s_fo = self.sem("fo")

            def f_fout(sync):
                s_fx.wait(sync, 2 * NFC)
                i = sync.dma_start(out=self.yT.rearrange("(c p) t -> p c t", p=128),
                                   in_=A.rearrange("p (c t) -> p c t", t=TOK))
                self.s_st.inc(i, 16)
            self.on("sync", f_fout)
            self.done(self.s_st)
        self.bar()
        self.release_scope()

    def build(self):
        with self.nc.Block() as block:
            self.block = block
            self.emit_setup()
            seq = [("ffn", 0), ("attn", 1), ("ffn", 2)][:self.depth_stop]
            for kind, li in seq:
                last = (li == len(seq) - 1 and li == self.depth_stop - 1)
                is_last = (li == self.depth_stop - 1)
                if kind == "ffn":
                    with self.nc.named_scope(f"ffn{li}"):
                        self.emit_ffn(li, is_last)
                else:
                    with self.nc.named_scope(f"attn{li}"):
                        self.emit_attn(is_last)
                    if not is_last:
                        pass
            if self.depth_stop == 2:
                pass
            st_n = self.s_st.n

            def f_fin(sync):
                self.s_st.wait_abs(sync, st_n)
            self.on("sync", f_fin)
        return self.nc


_CACHE = {}


def _get_nc(depth_stop=3):
    if depth_stop not in _CACHE:
        _CACHE[depth_stop] = MK(depth_stop).build()
    return _CACHE[depth_stop]


def kernel(**inputs):
    import ml_dtypes
    nc = _get_nc(3)
    x = np.asarray(inputs["L0_attn_x"])
    base = {
        "fWq": inputs["L0_ffn_Wq"], "fWk": inputs["L0_ffn_Wk"],
        "fWv": inputs["L0_ffn_Wv"], "fWo": inputs["L0_ffn_Wo"],
        "fbq": inputs["L0_ffn_bq"], "fbk": inputs["L0_ffn_bk"],
        "fbv": inputs["L0_ffn_bv"], "fbo": inputs["L0_ffn_bo"],
        "fWg": inputs["L0_ffn_Wg"], "fbg": inputs["L0_ffn_bg"],
        "fW1": inputs["L0_ffn_W1"], "fb1": inputs["L0_ffn_b1"],
        "fW2": inputs["L0_ffn_W2"], "fb2": inputs["L0_ffn_b2"],
        "aWq": inputs["L0_attn_Wq"], "aWkv": inputs["L0_attn_Wkv"],
        "aWo": inputs["L0_attn_Wo"],
        "anw": inputs["L0_attn_norm_w"], "anb": inputs["L0_attn_norm_b"],
        "aonw": inputs["L0_attn_out_norm_w"], "aonb": inputs["L0_attn_out_norm_b"],
        "eye": np.eye(128, dtype=np.float32),
    }
    base = {k: np.ascontiguousarray(np.asarray(v), dtype=np.float32)
            for k, v in base.items()}
    in_maps = []
    for r in range(NCORES):
        xs = np.ascontiguousarray(x[0, r * TOK:(r + 1) * TOK, :].T)
        in_maps.append({"xT": xs, **base})
    res = run_bass_kernel_spmd(nc, in_maps, core_ids=list(range(NCORES)))
    out = np.concatenate([res.results[r]["yT"].T for r in range(NCORES)], axis=0)
    return out.reshape(1, 16384, 1024).astype(np.float32)
